# version 10
# baseline (speedup 1.0000x reference)
import os
import numpy as np
import ml_dtypes
from contextlib import ExitStack

import concourse.bass as bass
import concourse.mybir as mybir
from concourse.bass_utils import run_bass_kernel_spmd

F32 = mybir.dt.float32
BF16 = mybir.dt.bfloat16
I32 = mybir.dt.int32
AF = mybir.ActivationFunctionType
ALU = mybir.AluOpType
AX = mybir.AxisListType

D = 2048
NT = 1024
H0, DH0 = 32, 64
H1, DH1 = 16, 128
NCH = 4
EPS = 1e-5


class Ch:
    def __init__(self, stack, nc, name, step=1):
        self.h = stack.enter_context(nc.semaphore(name))
        self.n = 0
        self.step = step


def build_nc():
    nc = bass.Bass("TRN2", target_bir_lowering=False)

    def din(name, shape, dt=F32):
        return nc.dram_tensor(name, shape, dt, kind="ExternalInput").ap()

    xo = din("xo", [NT, D])
    wA = [din(f"l0wA{i}", [16, 128, D]) for i in range(4)]
    wB = [din(f"l0wB{i}", [16, 4, 128, 512]) for i in range(4)]
    l0b = [din(f"l0b{i}", [D]) for i in range(4)]
    wg_in = din("wg_t", [16, 128, 16 * 512])
    wu_in = din("wu_t", [16, 128, 16 * 512])
    wd_in = din("wd_t", [16, 128, 64 * 128])
    w1_in = [din(f"l1w{i}", [16, 128, D]) for i in range(4)]
    gam = din("gam", [128])
    bet = din("bet", [128])
    wf1 = din("wf1", [16, 128, D])
    b1v = din("b1", [D])
    laT_in = din("laT", [128, 2048])
    lbS_in = din("lbS", [128, 2048])
    gwT_in = din("gwT", [128, 2048])
    gbv = din("gb", [8])
    ident_in = din("ident", [128, 128], BF16)
    mask_in = din("mask1", [4, 128, 256], BF16)
    sel8_in = din("sel8", [128, 128], BF16)

    yo = nc.dram_tensor("yo", [NT, D], F32, kind="ExternalOutput").ap()
    KDUMP = os.environ.get("KDUMP", "0") == "1"
    dmp = {}
    if KDUMP:
        for nm, shape, dt in [("d_xn", [128, 16 * NT], BF16),
                              ("d_res", [128, 16 * NT], BF16),
                              ("d_kvs", [H0, 128, NT], BF16),
                              ("d_qdr", [H0, DH0, NT], BF16),
                              ("d_ctx", [128, 16 * NT], BF16),
                              ("d_dff", [128, 16 * NT], BF16),
                              ("d_x1", [NT, D], F32),
                              ("d_kvs1", [H1, 256, NT], BF16),
                              ("d_qdr1", [H1, DH1, NT], BF16),
                              ("d_ctx1", [128, 16 * NT], BF16),
                              ("d_dff1", [128, 16 * NT], BF16),
                              ("d_dffB", [128, 16 * NT], BF16),
                              ("d_rk", [128, 16], F32),
                              ("d_div", [128, 256], BF16),
                              ("d_vaug", [128, 2048], BF16),
                              ("d_khb", [128, 2048], BF16),
                              ("d_zr", [1, NT], F32),
                              ("d_att", [128, 16 * 256], BF16),
                              ("d_cb", [128, 64], F32),
                              ("d_cmax", [128, 64], F32),
                              ("d_s", [128, 64], F32),
                              ("d_idx", [128, 64], F32),
                              ("d_tmax", [128, 4], F32)]:
            dmp[nm] = nc.dram_tensor(nm, shape, dt, kind="ExternalOutput").ap()

    kvs = nc.dram_tensor("kvs", [H0, 128, NT], BF16).ap()
    kvg = nc.dram_tensor("kvg", [2 * H0, 128, NT], BF16).ap()
    kvs1 = nc.dram_tensor("kvs1", [H1, 256, NT], BF16).ap()
    kvg1 = nc.dram_tensor("kvg1", [2 * H1, 256, NT], BF16).ap()
    qdr = nc.dram_tensor("qdr", [H0, DH0, NT], BF16).ap()
    qdr1 = nc.dram_tensor("qdr1", [H1, DH1, NT], BF16).ap()
    x1own = nc.dram_tensor("x1own", [NT, D], F32).ap()

    with ExitStack() as stack:
        def sb(name, shape, dt):
            return stack.enter_context(nc.sbuf_tensor(name, shape, dt))

        LI = Ch(stack, nc, "LI", 16)
        LW = Ch(stack, nc, "LW", 16)
        T = Ch(stack, nc, "T")
        V = Ch(stack, nc, "V")
        A = Ch(stack, nc, "A")
        G = Ch(stack, nc, "G")
        blk = stack.enter_context(nc.Block())
        seen = {e: {} for e in ("tensor", "vector", "scalar", "sync", "gpsimd")}
        OWN = {"tensor": T, "vector": V, "scalar": A, "sync": LI, "gpsimd": G}

        def step(engname, deps, fn, raw=False):
            own = OWN[engname]

            def body(eng):
                sn = seen[engname]
                alldeps = list(deps) + [(own, own.n)]
                for c, m in alldeps:
                    if m > 0 and m > sn.get(id(c), 0):
                        eng.wait_ge(c.h, m)
                        sn[id(c)] = m
                r = fn(eng)
                if raw:
                    r.then_inc(own.h, 1)
                else:
                    eng.drain().then_inc(own.h, 1)
                own.n += 1
            getattr(blk, engname)(body)
            return own.n

        def dma(engname, deps, pairs):
            ch = LI if engname == "sync" else LW

            def body(eng):
                sn = seen[engname]
                for c, m in deps:
                    if m > 0 and m > sn.get(id(c), 0):
                        eng.wait_ge(c.h, m)
                        sn[id(c)] = m
                for o, i in pairs:
                    eng.dma_start(out=o, in_=i).then_inc(ch.h, 16)
                    ch.n += 16
            getattr(blk, engname)(body)
            return ch.n

        def dma1(engname, deps, o, i):
            return dma(engname, deps, [(o, i)])

        pb = [stack.enter_context(nc.psum_tensor(f"pb{i}", [128, 512], F32))
              for i in range(6)]
        ptr = stack.enter_context(nc.psum_tensor("ptr", [128, 1024], BF16))

        MINSB = os.environ.get("MINSB", "0") == "1"
        bigA = sb("bigA", [128, 2048 if MINSB else 32768], BF16)
        if MINSB:
            res = ctxb = hspI = exps = khb = vhb = vaug = qhb = dtm = xnt = bigA
            sqb = vhb
        else:
            res = bigA[:, 0:16384]
            ctxb = bigA[:, 16384:32768]
            hspI = bigA[:, 0:32768]
            exps = bigA[:, 0:8192]
            khb = bigA[:, 8192:10240]
            vhb = bigA[:, 10240:12288]
            sqb = vhb
            vaug = bigA[:, 12288:14336]
            qhb = bigA[:, 14336:15360]
            dtm = bigA[:, 16384:18432]
            xnt = bigA[:, 16384:18432]

        ones_bf = sb("ones_bf", [128, 128], BF16)
        ident = sb("identsb", [128, 128], BF16)
        sel8 = sb("sel8sb", [128, 128], BF16)
        st0 = sb("st0", [128, 512], BF16)
        st0r = sb("st0r", [128, 128], BF16)
        gcol = sb("gcol", [128, 1], F32)
        bcol = sb("bcol", [128, 1], F32)
        masks = sb("masks", [128, 4 * 256], BF16)
        b1c = sb("b1c", [128, 16], F32)

        cbf32 = sb("cbf32", [128, 64], F32)
        cbF = sb("cbF", [128, 64], BF16)
        cbQ = sb("cbQ", [128, 64], F32)
        aQ = sb("aQ", [128, 64], F32)
        sQ = sb("sQ", [128, 64], F32)
        cmaxQ = sb("cmaxQ", [128, 64], F32)
        rmaxQ = sb("rmaxQ", [128, 1], F32)
        rminQ = sb("rminQ", [128, 1], F32)
        tmaxQ = sb("tmaxQ", [128, NCH], F32)
        idxQ = sb("idxQ", [128, 64], F32)
        accQ = sb("accQ", [128, 64], F32)
        thrQ = sb("thrQ", [128, 1], F32)
        tmxs = sb("tmxs", [128, 128], BF16)
        trow = sb("trow", [128, 128], BF16)
        trowf = sb("trowf", [1, 128], F32)
        qu = sb("qu", [128, 512], F32)
        qi = sb("qi", [128, 512], I32)
        qf = sb("qf", [128, 512], F32)

        wa_sb = sb("wa_sb", [128, 2048], F32)
        wdq = sb("wdq", [128, 2048], BF16)
        wsc = [sb(f"wsc{i}", [128, 1], F32) for i in range(4)]
        wdqB = sb("wdqB", [128, 16 * 128], BF16)
        wbo = sb("wbo", [128, 512], BF16)
        cbst = sb("cbst", [128, 128], BF16)
        biasb = sb("biasb", [128, 512], BF16)
        biasf = sb("biasf", [128, 64], F32)

        evst = sb("evst", [128, 1024], BF16)

        xn = sb("xn", [128, 128 if MINSB else 16 * NT], BF16)
        dff = sb("dff", [128, 128 if MINSB else 16 * NT], BF16)

        xt = sb("xt", [128, D], F32)
        bnst = sb("bnst", [128, 24], F32)
        bnag = sb("bnag", [128, 2], F32)
        rstd = sb("rstd", [128, 1], F32)
        nmu = sb("nmu", [128, 1], F32)
        tmpc = sb("tmpc", [128, 1], F32)

        wgs = sb("wgs", [128, 16 * 512], BF16)
        wus = sb("wus", [128, 16 * 512], BF16)
        wds = wus
        hh2 = [sb(f"h2b{i}", [128, 512], BF16) for i in range(2)]
        s2 = sb("s2", [128, 512], BF16)
        t4 = [evst[:, 512 + 128 * i:512 + 128 * (i + 1)] for i in range(4)]

        nbias = sb("nbias", [128, 1], F32)
        c85 = sb("c85", [128, 1], F32)
        c0 = sb("c0", [128, 1], F32)
        c1 = sb("c1", [128, 1], F32)
        rowb = wa_sb[0:1, 0:1024]
        srow = wa_sb[0:1, 0:512]
        zrow = rowb
        rk = sb("rk", [128, 16], F32)
        rkst = s2
        muq = qu[:, 0:512]
        wtok = [wdq[:, 128 * i:128 * (i + 1)] for i in range(8)]


        KSK0 = os.environ.get("KSKIP", "")
        if "identdma" in KSK0:
            m_ident = 0
        elif "maskdma" in KSK0:
            m_ident = dma("sync", [], [(ident[:], ident_in), (sel8[:], sel8_in)])
        else:
            m_ident = dma("sync", [], [
                (ident[:], ident_in),
                (sel8[:], sel8_in)]
                + [(masks[:, 256 * w:256 * (w + 1)], mask_in[w]) for w in range(4)])

        def finit(v):
            v.memset(ones_bf[:], 1.0)
            v.memset(st0[:], 0.0)
            v.memset(st0r[:], 0.0)
            v.memset(tmxs[:], 0.0)
            v.memset(c85[:], 8.5)
            for w in wtok:
                v.memset(w[:], 0.0)
        m_init = 0 if "finit" in KSK0 else step("vector", [], finit)

        KSKIP0 = os.environ.get("KSKIP", "")
        if "gb" in KSKIP0:
            m_gb = 0
        else:
            with nc.allow_non_contiguous_dma(reason="one-time [128,1] vector loads"):
                m_gb = dma("sync", [], [
                    (gcol[:], gam.rearrange("(p o) -> p o", o=1)),
                    (bcol[:], bet.rearrange("(p o) -> p o", o=1))])
        m_b1c = 0
        mtb = T.n
        for jj in ([] if "b1c" in os.environ.get("KSKIP", "") else range(16)):
            mlj = dma1("gpsimd", [(V, m_init), (T, mtb)], st0r[0:1, 0:128],
                       b1v.rearrange("(o n) -> o n", o=1)[0:1, 128 * jj:128 * (jj + 1)])
            mtb = step("tensor", [(LW, mlj), (LI, m_ident)],
                       lambda t: t.transpose(out=ptr[:, 0:128], in_=st0r[:],
                                             identity=ident[:]))
            m_b1c = step("vector", [(T, mtb)],
                         lambda v, jj=jj: v.tensor_copy(out=b1c[:, jj:jj + 1],
                                                        in_=ptr[:, 0:1]))
        m_init2 = 0 if "finit" in KSK0 else step("vector", [], lambda v: v.memset(st0r[:], 0.0))

        def ln_to_xn(src_dram, pre_deps):
            mv = V.n
            ma2 = A.n
            for i in range(8):
                ml = dma1("sync", [(V, mv), (A, ma2)] + pre_deps,
                          xt[:], src_dram[128 * i:128 * (i + 1), :])

                def fstats(v):
                    for j in range(4):
                        v.bn_stats(out=bnst[:, 6 * j:6 * j + 6],
                                   in_=xt[:, 512 * j:512 * (j + 1)])
                    v.drain()
                    v.bn_aggr(out=bnag[:], in_=bnst[:])
                    v.drain()
                    v.tensor_scalar(out=tmpc[:], in0=bnag[:, 1:2], scalar1=EPS,
                                    scalar2=None, op0=ALU.add)
                    v.drain()
                    v.reciprocal(out=rstd[:], in_=tmpc[:])
                mv1 = step("vector", [(LI, ml)], fstats)
                ma1 = step("scalar", [(V, mv1)],
                           lambda a: a.activation(out=rstd[:], in_=rstd[:],
                                                  func=AF.Sqrt))
                mv2 = step("vector", [(A, ma1)],
                           lambda v: v.tensor_scalar(out=nmu[:], in0=bnag[:, 0:1],
                                                     scalar1=rstd[:], scalar2=-1.0,
                                                     op0=ALU.mult, op1=ALU.mult))
                ma2 = step("scalar", [(V, mv2), (T, T.n), (V, V.n)],
                           lambda a: a.activation(out=xnt, in_=xt[:],
                                                  func=AF.Identity,
                                                  scale=rstd[:], bias=nmu[:]))
                for b in range(2):
                    def ftr(t, b=b):
                        for j in range(8):
                            k = 8 * b + j
                            t.transpose(out=ptr[:, 128 * j:128 * (j + 1)],
                                        in_=xnt[:, 128 * k:128 * (k + 1)],
                                        identity=ident[:])
                    mt = step("tensor", [(A, ma2), (V, V.n), (LI, m_ident)], ftr)

                    def fev(v, b=b, i=i):
                        for j in range(8):
                            k = 8 * b + j
                            v.tensor_copy(
                                out=xn[:, NT * k + 128 * i:NT * k + 128 * (i + 1)],
                                in_=ptr[:, 128 * j:128 * (j + 1)])
                    mv = step("vector", [(T, mt)], fev)
            return mv

        def quantize(src, dst):
            mstat = step("vector", [], lambda v: v.memset(tmxs[:], 0.0))
            for k in range(16):
                def fst(v, k=k):
                    for c in range(NCH):
                        sl = src[:, NT * k + 256 * c:NT * k + 256 * (c + 1)]
                        v.reduce_max(out=rmaxQ[:], in_=sl, axis=AX.X)
                        v.drain()
                        v.tensor_reduce(out=rminQ[:], in_=sl, axis=AX.X, op=ALU.min)
                        v.drain()
                        v.tensor_tensor(out=tmpc[:], in0=rmaxQ[:], in1=rminQ[:],
                                        op=ALU.add)
                        v.drain()
                        col = 4 * k + c
                        v.tensor_scalar(out=cbf32[:, col:col + 1], in0=tmpc[:],
                                        scalar1=0.5, scalar2=None, op0=ALU.mult)
                        v.drain()
                        v.tensor_copy(out=cbF[:, col:col + 1],
                                      in_=cbf32[:, col:col + 1])
                        v.drain()
                        v.tensor_scalar(out=qu[:, 256 * (c % 2):256 * (c % 2 + 1)],
                                        in0=sl, scalar1=cbf32[:, col:col + 1],
                                        scalar2=None, op0=ALU.subtract)
                        v.drain()
                        v.reduce_max(out=cmaxQ[:, col:col + 1],
                                     in_=qu[:, 256 * (c % 2):256 * (c % 2 + 1)],
                                     axis=AX.X, apply_absolute_value=True)
                        v.drain()
                mstat = step("vector", [], fst)
            mv = step("vector", [], lambda v: v.tensor_copy(out=tmxs[:, 0:64],
                                                            in_=cmaxQ[:]))
            mt = step("tensor", [(V, mv), (LI, m_ident), (V, V.n)],
                      lambda t: t.transpose(out=ptr[:, 0:128], in_=tmxs[:],
                                            identity=ident[:]))
            mv = step("vector", [(T, mt)],
                      lambda v: v.tensor_copy(out=trow[:], in_=ptr[:, 0:128]))
            mv = step("vector", [],
                      lambda v: v.tensor_reduce(out=tmxs[:, 0:1], in_=trow[:],
                                                axis=AX.X, op=ALU.max))
            mt = step("tensor", [(V, mv)],
                      lambda t: t.transpose(out=ptr[:, 0:128], in_=tmxs[:],
                                            identity=ident[:]))
            mv = step("vector", [(T, mt)],
                      lambda v: v.tensor_copy(out=trow[:], in_=ptr[:, 0:128]))

            def ftm2(v):
                g = trow[0:1, 0:64].rearrange("p (k o) -> p k o", o=4)
                for c in range(NCH):
                    v.tensor_reduce(out=trowf[:, c:c + 1], in_=g[:, :, c],
                                    axis=AX.X, op=ALU.max)
                v.drain()
                v.tensor_copy(out=st0r[0:1, 0:4], in_=trowf[:, 0:4])
            mv = step("vector", [], ftm2)
            mt = step("tensor", [(V, mv)],
                      lambda t: t.matmul(out=pb[5][:, 0:128], lhsT=ones_bf[:],
                                         rhs=st0r[:], start=True, stop=True))
            mv = step("vector", [(T, mt)],
                      lambda v: v.tensor_copy(out=tmaxQ[:], in_=pb[5][:, 0:4]))

            def fscale(v):
                v.memset(idxQ[:], 0.0)
                v.drain()
                cm = cmaxQ[:, 0:64].rearrange("p (k o) -> p k o", o=4)
                ix = idxQ[:, 0:64].rearrange("p (k o) -> p k o", o=4)
                for c in range(NCH):
                    for j in range(7):
                        v.tensor_scalar(out=thrQ[:], in0=tmaxQ[:, c:c + 1],
                                        scalar1=float(2.0 ** (j - 7)), scalar2=None,
                                        op0=ALU.mult)
                        v.drain()
                        v.tensor_scalar(out=accQ[:, 0:16], in0=cm[:, :, c],
                                        scalar1=thrQ[:], scalar2=None, op0=ALU.is_gt)
                        v.drain()
                        v.tensor_tensor(out=ix[:, :, c], in0=ix[:, :, c],
                                        in1=accQ[:, 0:16], op=ALU.add)
                        v.drain()
                v.memset(sQ[:], 0.0)
                v.drain()
                for jj in range(8):
                    v.scalar_tensor_tensor(out=accQ[:], in0=idxQ[:], scalar=float(jj),
                                           in1=sQ[:], op0=ALU.is_equal, op1=ALU.bypass)
                    v.drain()
                    v.tensor_scalar(out=accQ[:], in0=accQ[:],
                                    scalar1=float(2.0 ** (jj - 7)), scalar2=None,
                                    op0=ALU.mult)
                    v.drain()
                    v.tensor_tensor(out=sQ[:], in0=sQ[:], in1=accQ[:], op=ALU.add)
                    v.drain()
                sv = sQ[:, 0:64].rearrange("p (k o) -> p k o", o=4)
                for c in range(NCH):
                    v.tensor_scalar(out=sv[:, :, c], in0=sv[:, :, c],
                                    scalar1=tmaxQ[:, c:c + 1],
                                    scalar2=float(1.0 / 7.0),
                                    op0=ALU.mult, op1=ALU.mult)
                v.drain()
                v.tensor_scalar(out=sQ[:], in0=sQ[:], scalar1=1e-9, scalar2=None,
                                op0=ALU.max)
                v.drain()
                v.reciprocal(out=aQ[:], in_=sQ[:])
                v.drain()
                v.tensor_tensor(out=cbQ[:], in0=cbf32[:], in1=aQ[:], op=ALU.mult)
                v.drain()
                v.tensor_scalar(out=cbQ[:], in0=cbQ[:], scalar1=-1.0, scalar2=None,
                                op0=ALU.mult)
            mv = step("vector", [], fscale)

            mg = G.n
            mvf = mv
            for k in range(16):
                for cp in range(2):
                    def fu(v, k=k, cp=cp):
                        for c2 in range(2):
                            c = 2 * cp + c2
                            col = 4 * k + c
                            v.tensor_scalar(out=qu[:, 256 * c2:256 * (c2 + 1)],
                                            in0=src[:, NT * k + 256 * c:NT * k + 256 * (c + 1)],
                                            scalar1=aQ[:, col:col + 1],
                                            scalar2=cbQ[:, col:col + 1],
                                            op0=ALU.mult, op1=ALU.add)
                    mvk = step("vector", [], fu)

                    def fcvt(v):
                        v.tensor_copy(out=qi[:], in_=qu[:])
                        v.drain()
                        v.tensor_copy(out=qf[:], in_=qi[:])
                    mg2 = step("vector", [], fcvt)

                    def ff(v, k=k, cp=cp):
                        for c2 in range(2):
                            c = 2 * cp + c2
                            col = 4 * k + c
                            v.tensor_scalar(
                                out=dst[:, NT * k + 256 * c:NT * k + 256 * (c + 1)],
                                in0=qf[:, 256 * c2:256 * (c2 + 1)],
                                scalar1=sQ[:, col:col + 1], scalar2=None,
                                op0=ALU.mult)
                    mvf = step("vector", [], ff)
            return mvf

        def cb_bias_rows(wB_d, bias_d, scaleq):
            mE = V.n
            for g in range(4):
                mmm = T.n
                for k in range(16):
                    def fstage(v, k=k):
                        v.memset(cbst[:], 0.0)
                        v.drain()
                        v.tensor_copy(out=cbst[:, 0:4], in_=cbF[:, 4 * k:4 * k + 4])
                    mvS = step("vector", [(T, mmm)], fstage)
                    mgW = dma1("gpsimd", [(T, mmm)], wbo[:], wB_d[k, g])
                    mmm = step("tensor", [(V, mvS), (LW, mgW)],
                               lambda t, k=k: t.matmul(out=pb[4][:], lhsT=cbst[:],
                                                       rhs=wbo[:], start=(k == 0),
                                                       stop=False))
                mlb = dma1("gpsimd", [(T, mmm), (V, V.n)], st0[0:1, :],
                           bias_d.rearrange("(o n) -> o n", o=1)[0:1, 512 * g:512 * (g + 1)])
                mmm = step("tensor", [(LW, mlb)],
                           lambda t: t.matmul(out=pb[4][:], lhsT=ones_bf[:],
                                              rhs=st0[:], start=False, stop=True))
                mvE = step("vector", [(T, mmm)],
                           lambda v: v.tensor_copy(out=biasb[0:4, :],
                                                   in_=pb[4][0:4, :]))
                for i in range(4):
                    oc = 4 * g + i
                    mt = step("tensor", [(V, mvE)],
                              lambda t, i=i: t.transpose(
                                  out=ptr[:, 0:128],
                                  in_=biasb[:, 128 * i:128 * (i + 1)],
                                  identity=ident[:]))
                    sc = 0.125 if scaleq else 1.0
                    mE = step("vector", [(T, mt)],
                              lambda v, oc=oc, sc=sc: v.tensor_scalar(
                                  out=biasf[:, 4 * oc:4 * oc + 4],
                                  in0=ptr[:, 0:4], scalar1=sc, scalar2=None,
                                  op0=ALU.mult))
            return mE

        def wslab_quant(wa_dram_j):
            ml = dma1("sync", [(V, V.n), (A, A.n)], wa_sb[:], wa_dram_j)

            def fstat(v):
                v.reduce_max(out=wsc[0][:], in_=wa_sb[:], axis=AX.X,
                             apply_absolute_value=True)
                v.drain()
                v.tensor_scalar(out=wsc[1][:], in0=wsc[0][:], scalar1=7e-9,
                                scalar2=float(1.0 / 7.0), op0=ALU.max, op1=ALU.mult)
                v.drain()
                v.reciprocal(out=wsc[2][:], in_=wsc[1][:])
                v.drain()
                v.tensor_scalar(out=wsc[3][:], in0=wsc[1][:], scalar1=-8.0,
                                scalar2=None, op0=ALU.mult)
            mv = step("vector", [(LI, ml), (G, G.n)], fstat)
            mvE = V.n
            for hh in range(4):
                ma = step("scalar", [(V, mv), (V, V.n)],
                          lambda a, hh=hh: a.activation(
                              out=qu[:], in_=wa_sb[:, 512 * hh:512 * (hh + 1)],
                              func=AF.Identity, scale=wsc[2][:]))

                def fcvt(v):
                    v.tensor_copy(out=qi[:], in_=qu[:])
                    v.drain()
                    v.tensor_copy(out=qf[:], in_=qi[:])
                mg = step("vector", [(A, ma)], fcvt)
                ma2 = step("scalar", [(V, mg), (T, T.n)],
                           lambda a, hh=hh: a.activation(
                               out=wdq[:, 512 * hh:512 * (hh + 1)], in_=qf[:],
                               func=AF.Identity, scale=wsc[1][:]))

                def ftr(t, hh=hh):
                    for j in range(4):
                        k = 4 * hh + j
                        t.transpose(out=ptr[:, 128 * j:128 * (j + 1)],
                                    in_=wdq[:, 128 * k:128 * (k + 1)],
                                    identity=ident[:])
                mt = step("tensor", [(A, ma2), (V, mvE)], ftr)

                def fev(v, hh=hh):
                    for j in range(4):
                        k = 4 * hh + j
                        v.tensor_copy(out=wdqB[:, 128 * k:128 * (k + 1)],
                                      in_=ptr[:, 128 * j:128 * (j + 1)])
                mvE = step("vector", [(T, mt)], fev)
            return mvE

        STOP = int(os.environ.get("KSTOP", "9"))

        def dump_sb(name, src_ap, deps):
            if KDUMP and name in dmp:
                dma1("sync", deps, dmp[name], src_ap)

        def dump_dr(name, src_ap, deps):
            if KDUMP and name in dmp:
                dma1("sync", deps, dmp[name], src_ap)

        def finish_early():
            mo = 0
            for i in range(8):
                mlx = dma1("sync", [(V, V.n), (A, A.n)], xt[:],
                           xo[128 * i:128 * (i + 1), :])
                mo = dma1("sync", [(LI, mlx)], yo[128 * i:128 * (i + 1), :], xt[:])

            def ffin(s_):
                s_.wait_ge(LI.h, mo)
            blk.sync(ffin)

        KSKIP = os.environ.get("KSKIP", "")
        if "ln" in KSKIP:
            m_xn = V.n
        else:
            m_xn = ln_to_xn(xo, [(LI, m_ident), (V, m_init2)])
        if "quant" in KSKIP:
            m_q = V.n
        else:
            m_q = quantize(xn, res)
        dump_sb("d_xn", xn[:], [(V, m_xn)])
        dump_sb("d_res", res, [(V, m_q)])
        dump_sb("d_cb", cbf32[:], [(V, m_q)])
        dump_sb("d_cmax", cmaxQ[:], [(V, m_q)])
        dump_sb("d_s", sQ[:], [(V, m_q)])
        dump_sb("d_idx", idxQ[:], [(V, m_q)])
        dump_sb("d_tmax", tmaxQ[:], [(V, m_q)])
        if STOP <= 1:
            finish_early()
            return nc

        def l0_proj(pi, kind):
            mB = cb_bias_rows(wB[pi], l0b[pi], scaleq=(kind == "q"))
            mo = 0
            for j in range(16):
                mW = wslab_quant(wA[pi][j])

                def fmm(t, j=j):
                    for grp in range(2):
                        for k in range(16):
                            t.matmul(out=pb[grp][:],
                                     lhsT=wdqB[:, 128 * k:128 * (k + 1)],
                                     rhs=res[:, NT * k + 512 * grp:NT * k + 512 * (grp + 1)],
                                     start=(k == 0), stop=(k == 15))
                mt = step("tensor", [(V, mW), (V, m_q), (V, mB), (A, A.n)], fmm)

                def fev(a, j=j):
                    sc = 0.125 if kind == "q" else 1.0
                    for grp in range(2):
                        for c2 in range(2):
                            c = 2 * grp + c2
                            a.activation(
                                out=evst[:, 512 * grp + 256 * c2:512 * grp + 256 * (c2 + 1)],
                                in_=pb[grp][:, 256 * c2:256 * (c2 + 1)],
                                func=AF.Identity, scale=sc,
                                bias=biasf[:, 4 * j + c:4 * j + c + 1])
                ma = step("scalar", [(T, mt), (LI, LI.n)], fev)
                if kind == "q":
                    mo = dma("sync", [(A, ma)],
                             [(qdr[2 * j, :, :], evst[0:64, :]),
                              (qdr[2 * j + 1, :, :], evst[64:128, :])])
                else:
                    r0 = 0 if kind == "k" else 64
                    mo = dma("sync", [(A, ma)],
                             [(kvs[2 * j, r0:r0 + 64, :], evst[0:64, :]),
                              (kvs[2 * j + 1, r0:r0 + 64, :], evst[64:128, :])])
            return mo

        if "proj" not in KSKIP:
            l0_proj(0, "q")
            l0_proj(1, "k")
            l0_proj(2, "v")
        m_kv = LI.n
        dump_dr("d_kvs", kvs, [(LI, m_kv)])
        dump_dr("d_qdr", qdr, [(LI, m_kv)])
        m_kv = LI.n
        if STOP <= 2:
            finish_early()
            return nc

        KCC = os.environ.get("KCC", "full")
        if KCC == "tiny":
            ccs = nc.dram_tensor("ccs", [128, 256], F32).ap()
            ccg = nc.dram_tensor("ccg", [256, 256], F32).ap()
            m_cc0 = step("gpsimd", [(LI, m_kv)],
                         lambda g: g.collective_compute(
                             "AllGather", ALU.bypass,
                             replica_groups=[[0, 1], [2, 3], [4, 5], [6, 7]],
                             ins=[ccs], outs=[ccg]))
        elif KCC == "none":
            m_cc0 = step("gpsimd", [(LI, m_kv)], lambda g: g.engine_nop())
        else:
            m_cc0 = step("gpsimd", [(LI, m_kv)],
                         lambda g: g.collective_compute(
                             "AllGather", ALU.bypass,
                             replica_groups=[[0, 1], [2, 3], [4, 5], [6, 7]],
                             ins=[kvs], outs=[kvg]), raw=True)
        if STOP == 25:
            mwait = step("gpsimd", [(G, m_cc0)], lambda g: g.engine_nop())
            finish_early()
            return nc
        if STOP == 26:
            finish_early()
            return nc

        def l0_ffn():
            mdff = V.n
            for ih in range(2):
                for jsl in range(8):
                    js = 8 * ih + jsl
                    mw = dma("gpsimd", [(T, T.n)],
                             [(wgs[:], wg_in[js]), (wus[:], wu_in[js])])
                    for tt in range(8):
                        def fmm(t, tt=tt):
                            for k in range(16):
                                lhs = xn[:, NT * k + 128 * tt:NT * k + 128 * (tt + 1)]
                                t.matmul(out=pb[0][:], lhsT=lhs,
                                         rhs=wgs[:, 512 * k:512 * (k + 1)],
                                         start=(k == 0), stop=(k == 15))
                                t.matmul(out=pb[1][:], lhsT=lhs,
                                         rhs=wus[:, 512 * k:512 * (k + 1)],
                                         start=(k == 0), stop=(k == 15))
                        mt = step("tensor", [(LW, mw), (V, m_xn), (A, A.n),
                                             (V, V.n)], fmm)
                        ma = step("scalar", [(T, mt), (V, V.n), (LI, LI.n)],
                                  lambda a: a.activation(out=evst[:, 0:512],
                                                         in_=pb[0][:],
                                                         func=AF.Silu))
                        mvh = step("vector", [(A, ma)],
                                   lambda v, tt=tt: v.tensor_tensor(
                                       out=hh2[tt % 2][:],
                                       in0=pb[1][:], in1=evst[:, 0:512],
                                       op=ALU.mult))
                        ma2 = step("scalar", [(V, mvh)],
                                   lambda a, tt=tt: a.activation(
                                       out=s2[:], in_=hh2[tt % 2][:],
                                       func=AF.Square))

                        def fsp(v, tt=tt):
                            q = [s2[:, 128 * i:128 * (i + 1)] for i in range(4)]
                            v.tensor_tensor(out=t4[0][:], in0=q[0], in1=q[1], op=ALU.max)
                            v.tensor_tensor(out=t4[1][:], in0=q[0], in1=q[1], op=ALU.min)
                            v.tensor_tensor(out=t4[2][:], in0=q[2], in1=q[3], op=ALU.max)
                            v.tensor_tensor(out=t4[3][:], in0=q[2], in1=q[3], op=ALU.min)
                            v.drain()
                            v.tensor_tensor(out=t4[0][:], in0=t4[0][:], in1=t4[2][:],
                                            op=ALU.min)
                            v.tensor_tensor(out=t4[1][:], in0=t4[1][:], in1=t4[3][:],
                                            op=ALU.max)
                            v.drain()
                            v.tensor_tensor(out=t4[0][:], in0=t4[0][:], in1=t4[1][:],
                                            op=ALU.max)
                            v.drain()
                            for i in range(4):
                                v.tensor_tensor(out=t4[2][:],
                                                in0=s2[:, 128 * i:128 * (i + 1)],
                                                in1=t4[0][:], op=ALU.is_ge)
                                v.drain()
                                v.tensor_tensor(out=hh2[tt % 2][:, 128 * i:128 * (i + 1)],
                                                in0=hh2[tt % 2][:, 128 * i:128 * (i + 1)],
                                                in1=t4[2][:], op=ALU.mult)
                                v.drain()
                        mv = step("vector", [(A, ma2)], fsp)

                        def ftr(t, tt=tt):
                            for i in range(4):
                                t.transpose(out=ptr[:, 128 * i:128 * (i + 1)],
                                            in_=hh2[tt % 2][:, 128 * i:128 * (i + 1)],
                                            identity=ident[:])
                        mt2 = step("tensor", [(V, mv), (V, V.n)], ftr)

                        def fev(v, jsl=jsl, tt=tt):
                            for i in range(4):
                                ktl = 4 * jsl + i
                                v.tensor_copy(
                                    out=hspI[:, 1024 * ktl + 128 * tt:1024 * ktl + 128 * (tt + 1)],
                                    in_=ptr[:, 128 * i:128 * (i + 1)])
                        step("vector", [(T, mt2)], fev)
                m_hsp = V.n
                for oc in range(16):
                    mw = dma1("gpsimd", [(T, T.n)], wds[:, 0:4096],
                              wd_in[oc][:, 4096 * ih:4096 * (ih + 1)])

                    def fmm(t, oc=oc):
                        for ktl in range(32):
                            for grp in range(2):
                                t.matmul(out=pb[2 + grp][:],
                                         lhsT=wds[:, 128 * ktl:128 * (ktl + 1)],
                                         rhs=hspI[:, 1024 * ktl + 512 * grp:1024 * ktl + 512 * (grp + 1)],
                                         start=(ktl == 0), stop=(ktl == 31))
                    mt = step("tensor", [(LW, mw), (V, m_hsp), (V, V.n)], fmm)

                    def fev(v, oc=oc, ih=ih):
                        for grp in range(2):
                            dsl = dff[:, NT * oc + 512 * grp:NT * oc + 512 * (grp + 1)]
                            if ih == 0:
                                v.tensor_copy(out=dsl, in_=pb[2 + grp][:])
                            else:
                                v.tensor_tensor(out=dsl, in0=dsl, in1=pb[2 + grp][:],
                                                op=ALU.add)
                    mdff = step("vector", [(T, mt)], fev)
            return mdff

        m_dff = l0_ffn()
        dump_sb("d_dff", dff[:], [(V, m_dff)])
        if STOP <= 3:
            finish_early()
            return nc

        def l0_mix():
            def fz(v):
                v.memset(khb[64:128, :], 0.0)
                v.memset(vhb[64:128, :], 0.0)
                v.memset(qhb[64:128, :], 0.0)
            step("vector", [(T, T.n), (A, A.n)], fz)
            mctx = V.n
            for h in range(H0):
                mlk = dma("sync", [(V, V.n), (T, T.n), (G, m_cc0), (A, A.n),
                                   (LI, m_kv)],
                          [(khb[0:64, 1024 * r:1024 * (r + 1)], kvg[32 * r + h, 0:64, :])
                           for r in range(2)]
                          + [(vhb[0:64, 1024 * r:1024 * (r + 1)], kvg[32 * r + h, 64:128, :])
                             for r in range(2)]
                          + [(qhb[0:64, :], qdr[h])])
                mvV = V.n
                for b in range(2):
                    def ftr(t, b=b):
                        for i in range(8):
                            kt = 8 * b + i
                            t.transpose(out=ptr[:, 128 * i:128 * (i + 1)],
                                        in_=vhb[:, 128 * kt:128 * (kt + 1)],
                                        identity=ident[:])
                    mt = step("tensor", [(LI, mlk), (V, mvV)], ftr)

                    def fevV(v, b=b):
                        for i in range(8):
                            kt = 8 * b + i
                            v.tensor_copy(out=vaug[:, 128 * kt:128 * kt + 64],
                                          in_=ptr[:, 128 * i:128 * i + 64])
                    mvV = step("vector", [(T, mt)], fevV)

                def fones(v):
                    for kt in range(16):
                        v.memset(vaug[:, 128 * kt + 64:128 * kt + 65], 1.0)
                mvV = step("vector", [(V, mvV)], fones)
                maS = step("scalar", [(LI, mlk), (T, T.n), (V, mvV)],
                           lambda a: a.activation(out=sqb[:], in_=khb[:],
                                                  func=AF.Square))
                mvn = V.n
                for g in range(4):
                    mt = step("tensor", [(A, maS), (V, mvn)],
                              lambda t, g=g: t.matmul(out=pb[4][:], lhsT=ones_bf[:],
                                                      rhs=sqb[:, 512 * g:512 * (g + 1)],
                                                      start=True, stop=True))

                    def fmx(v, g=g):
                        v.reduce_max(out=c1[0:1, :], in_=pb[4][0:1, :], axis=AX.X)
                        v.drain()
                        if g == 0:
                            v.tensor_copy(out=c0[0:1, :], in_=c1[0:1, :])
                        else:
                            v.tensor_tensor(out=c0[0:1, :], in0=c0[0:1, :],
                                            in1=c1[0:1, :], op=ALU.max)
                    mvn = step("vector", [(T, mt)], fmx)
                maS = step("scalar", [(LI, mlk), (T, T.n)],
                           lambda a: a.activation(out=sqb[:, 0:1024], in_=qhb[:],
                                                  func=AF.Square))
                for g in range(2):
                    mt = step("tensor", [(A, maS), (V, mvn)],
                              lambda t, g=g: t.matmul(out=pb[4][:], lhsT=ones_bf[:],
                                                      rhs=sqb[:, 512 * g:512 * (g + 1)],
                                                      start=True, stop=True))

                    def fmx2(v, g=g):
                        v.reduce_max(out=c1[0:1, :], in_=pb[4][0:1, :], axis=AX.X)
                        v.drain()
                        if g == 0:
                            v.tensor_copy(out=tmpc[0:1, :], in_=c1[0:1, :])
                        else:
                            v.tensor_tensor(out=tmpc[0:1, :], in0=tmpc[0:1, :],
                                            in1=c1[0:1, :], op=ALU.max)
                    mvn = step("vector", [(T, mt)], fmx2)

                def fB(v):
                    v.tensor_tensor(out=c0[0:1, :], in0=c0[0:1, :], in1=tmpc[0:1, :],
                                    op=ALU.mult)
                    v.drain()
                    v.tensor_copy(out=st0r[0:1, 0:1], in_=c0[0:1, :])
                mvn = step("vector", [], fB)
                mt = step("tensor", [(V, mvn)],
                          lambda t: t.matmul(out=pb[4][:, 0:128], lhsT=ones_bf[:],
                                             rhs=st0r[:], start=True, stop=True))
                mvn = step("vector", [(T, mt)],
                           lambda v: v.tensor_copy(out=c0[:], in_=pb[4][:, 0:1]))
                ma3 = step("scalar", [(V, mvn)],
                           lambda a: a.activation(out=nbias[:], in_=c0[:],
                                                  func=AF.Sqrt))
                mvn = step("vector", [(A, ma3)],
                           lambda v: v.tensor_scalar(out=nbias[:], in0=nbias[:],
                                                     scalar1=-1.0, scalar2=None,
                                                     op0=ALU.mult))
                for qt in range(2):
                    mexp = A.n
                    for kt in range(16):
                        bank = kt % 2
                        mt = step("tensor", [(LI, mlk), (V, mvn), (A, mexp)],
                                  lambda t, kt=kt, qt=qt, bank=bank: t.matmul(
                                      out=pb[bank][:],
                                      lhsT=khb[:, 128 * kt:128 * (kt + 1)],
                                      rhs=qhb[:, 512 * qt:512 * (qt + 1)],
                                      start=True, stop=True))
                        mexp = step("scalar", [(T, mt)],
                                    lambda a, kt=kt, bank=bank: a.activation(
                                        out=exps[:, 512 * kt:512 * (kt + 1)],
                                        in_=pb[bank][:], func=AF.Exp,
                                        bias=nbias[:], scale=1.0))

                    def fpv(t):
                        for kt in range(16):
                            t.matmul(out=pb[2][0:65, :],
                                     lhsT=vaug[:, 128 * kt:128 * kt + 65],
                                     rhs=exps[:, 512 * kt:512 * (kt + 1)],
                                     start=(kt == 0), stop=(kt == 15))
                    mt = step("tensor", [(A, mexp), (V, mvV), (V, V.n)], fpv)

                    def frc(v):
                        v.reciprocal(out=srow[0:1, :], in_=pb[2][64:65, :])
                        v.drain()
                        v.tensor_copy(out=st0[0:1, :], in_=srow[0:1, :])
                    mv1 = step("vector", [(T, mt)], frc)
                    mt2 = step("tensor", [(V, mv1)],
                               lambda t: t.matmul(out=pb[3][:], lhsT=ones_bf[:],
                                                  rhs=st0[:], start=True, stop=True))
                    mv2 = step("vector", [(T, mt2)],
                               lambda v: v.tensor_copy(out=rkst[:], in_=pb[3][:]))

                    def fctx(v, h=h, qt=qt):
                        fc = h // 2
                        ro = 64 * (h % 2)
                        v.tensor_tensor(
                            out=ctxb[ro:ro + 64, NT * fc + 512 * qt:NT * fc + 512 * (qt + 1)],
                            in0=pb[2][0:64, :], in1=rkst[0:64, :], op=ALU.mult)
                    mctx = step("vector", [(V, mv2)], fctx)
            return mctx

        m_ctx = l0_mix()
        dump_sb("d_ctx", ctxb, [(V, m_ctx)])

        m_q2 = quantize(ctxb, res)
        mB_o = cb_bias_rows(wB[3], l0b[3], scaleq=False)

        def l0_out():
            mE = V.n
            for j in range(16):
                mW = wslab_quant(wA[3][j])

                def fmm(t, j=j):
                    for grp in range(2):
                        for k in range(16):
                            t.matmul(out=pb[grp][:],
                                     lhsT=wdqB[:, 128 * k:128 * (k + 1)],
                                     rhs=res[:, NT * k + 512 * grp:NT * k + 512 * (grp + 1)],
                                     start=(k == 0), stop=(k == 15))
                mt = step("tensor", [(V, mW), (V, m_q2), (V, mB_o), (A, A.n),
                                     (V, V.n)], fmm)

                def fev(v, j=j):
                    for grp in range(2):
                        for c2 in range(2):
                            c = 2 * grp + c2
                            sl = dff[:, NT * j + 512 * grp + 256 * c2:NT * j + 512 * grp + 256 * (c2 + 1)]
                            v.scalar_tensor_tensor(
                                out=sl, in0=pb[grp][:, 256 * c2:256 * (c2 + 1)],
                                scalar=biasf[:, 4 * j + c:4 * j + c + 1],
                                in1=sl, op0=ALU.add, op1=ALU.add)
                mE = step("vector", [(T, mt), (V, m_dff), (LI, LI.n)], fev)
            mo = 0
            for i in range(8):
                mlx = dma1("sync", [(V, V.n), (A, A.n)], xt[:],
                           xo[128 * i:128 * (i + 1), :])
                mE2 = V.n
                for b in range(2):
                    def ftr(t, i=i, b=b):
                        for j in range(8):
                            k = 8 * b + j
                            t.transpose(out=ptr[:, 128 * j:128 * (j + 1)],
                                        in_=dff[:, NT * k + 128 * i:NT * k + 128 * (i + 1)],
                                        identity=ident[:])
                    mt = step("tensor", [(V, mE), (V, V.n)], ftr)

                    def fevd(v, b=b):
                        for j in range(8):
                            k = 8 * b + j
                            v.tensor_copy(out=dtm[:, 128 * k:128 * (k + 1)],
                                          in_=ptr[:, 128 * j:128 * (j + 1)])
                    mE2 = step("vector", [(T, mt)], fevd)
                mv = step("vector", [(V, mE2), (LI, mlx)],
                          lambda v: v.tensor_tensor(out=xt[:], in0=xt[:],
                                                    in1=dtm, op=ALU.add))
                mo = dma1("sync", [(V, mv)], x1own[128 * i:128 * (i + 1), :], xt[:])
            return mo

        m_x1 = l0_out()
        dump_dr("d_x1", x1own, [(LI, m_x1)])
        m_x1 = LI.n if KDUMP else m_x1
        if STOP <= 4:
            finish_early()
            return nc

        m_xn1 = ln_to_xn(x1own, [(LI, m_x1)])

        def l1_proj(pi, kind):
            mo = 0
            for j in range(16):
                mw = dma1("gpsimd", [(T, T.n)], wgs[:, 0:2048], w1_in[pi][j])

                def fmm(t, j=j):
                    for grp in range(2):
                        for k in range(16):
                            t.matmul(out=pb[grp][:],
                                     lhsT=wgs[:, 128 * k:128 * (k + 1)],
                                     rhs=xn[:, NT * k + 512 * grp:NT * k + 512 * (grp + 1)],
                                     start=(k == 0), stop=(k == 15))
                mt = step("tensor", [(LW, mw), (V, m_xn1), (A, A.n), (V, V.n)], fmm)
                if kind == "v":
                    def fevv(a):
                        a.activation(out=evst[:, 0:512], in_=pb[0][:],
                                     func=AF.Identity)
                        a.drain()
                        a.activation(out=evst[:, 512:1024], in_=pb[1][:],
                                     func=AF.Identity)
                    ma = step("scalar", [(T, mt), (LI, LI.n)], fevv)
                    mo = dma1("sync", [(A, ma)], kvs1[j, 128:256, :], evst[:])
                elif kind == "k":
                    def fevk(a):
                        a.activation(out=evst[:, 0:512], in_=pb[0][:],
                                     func=AF.Identity, scale=gcol[:], bias=bcol[:])
                        a.drain()
                        a.activation(out=evst[:, 512:1024], in_=pb[1][:],
                                     func=AF.Identity, scale=gcol[:], bias=bcol[:])
                    ma = step("scalar", [(T, mt), (LI, max(m_gb, LI.n))], fevk)
                    mo = dma1("sync", [(A, ma)], kvs1[j, 0:128, :], evst[:])
                else:
                    def fevq(a):
                        a.activation(out=evst[:, 0:512], in_=pb[0][:],
                                     func=AF.Identity, scale=gcol[:], bias=bcol[:])
                        a.drain()
                        a.activation(out=evst[:, 512:1024], in_=pb[1][:],
                                     func=AF.Identity, scale=gcol[:], bias=bcol[:])
                    ma = step("scalar", [(T, mt), (LI, max(m_gb, LI.n))], fevq)
                    mvq = V.n
                    for grp in range(2):
                        mt2 = step("tensor", [(A, ma), (V, V.n)],
                                   lambda t, grp=grp: t.matmul(
                                       out=pb[2][:], lhsT=ones_bf[:],
                                       rhs=evst[:, 512 * grp:512 * (grp + 1)],
                                       start=True, stop=True))
                        mvq = step("vector", [(T, mt2)],
                                   lambda v, grp=grp: v.scalar_tensor_tensor(
                                       out=evst[:, 512 * grp:512 * (grp + 1)],
                                       in0=pb[2][:], scalar=float(-1.0 / 128.0),
                                       in1=evst[:, 512 * grp:512 * (grp + 1)],
                                       op0=ALU.mult, op1=ALU.add))
                    mo = dma1("sync", [(V, mvq), (A, A.n)], qdr1[j], evst[:])
            return mo

        l1_proj(0, "q")
        l1_proj(1, "k")
        l1_proj(2, "v")
        m_kv1 = LI.n
        dump_dr("d_kvs1", kvs1, [(LI, m_kv1)])
        dump_dr("d_qdr1", qdr1, [(LI, m_kv1)])
        m_kv1 = LI.n

        m_cc1 = step("gpsimd", [(LI, m_kv1)],
                     lambda g: g.collective_compute(
                         "AllGather", ALU.bypass,
                         replica_groups=[[0, 1], [2, 3], [4, 5], [6, 7]],
                         ins=[kvs1], outs=[kvg1]), raw=True)

        def l1_base():
            mE = V.n
            for j in range(16):
                mw = dma1("gpsimd", [(T, T.n)], wgs[:, 0:2048], wf1[j])

                def fmm(t, j=j):
                    for grp in range(2):
                        for k in range(16):
                            t.matmul(out=pb[grp][:],
                                     lhsT=wgs[:, 128 * k:128 * (k + 1)],
                                     rhs=xn[:, NT * k + 512 * grp:NT * k + 512 * (grp + 1)],
                                     start=(k == 0), stop=(k == 15))
                mt = step("tensor", [(LW, mw), (V, m_xn1), (A, A.n), (V, V.n)], fmm)

                def fev(v, j=j):
                    for grp in range(2):
                        v.tensor_scalar(
                            out=dff[:, NT * j + 512 * grp:NT * j + 512 * (grp + 1)],
                            in0=pb[grp][:], scalar1=b1c[:, j:j + 1], scalar2=None,
                            op0=ALU.add)
                mE = step("vector", [(T, mt), (V, m_b1c)], fev)
            return mE

        m_base = l1_base()
        dump_sb("d_dffB", dff[:], [(V, m_base)])

        def l1_gate_lora():
            mgw = dma1("gpsimd", [(T, T.n)], wds[:, 0:2048], gwT_in)
            mla = dma1("gpsimd", [(T, T.n)], wds[:, 2048:4096], laT_in)
            mlb2 = dma1("gpsimd", [(T, T.n)], wds[:, 4096:6144], lbS_in)
            mgbs = step("vector", [(LI, LI.n), (T, T.n)],
                        lambda v: v.memset(st0r[:], 0.0))
            mgbl = dma1("gpsimd", [(V, mgbs)], st0r[0:1, 0:8],
                        gbv.rearrange("(o n) -> o n", o=1))
            for tt in range(8):
                def fmm(t, tt=tt):
                    for k in range(16):
                        t.matmul(out=pb[4][:, 0:128],
                                 lhsT=xn[:, NT * k + 128 * tt:NT * k + 128 * (tt + 1)],
                                 rhs=wds[:, 128 * k:128 * (k + 1)],
                                 start=(k == 0), stop=False)
                    t.matmul(out=pb[4][:, 0:128], lhsT=ones_bf[:],
                             rhs=st0r[:], start=False, stop=True)
                mt = step("tensor", [(LW, max(mgw, mgbl)), (V, m_xn1), (V, V.n),
                                     (A, A.n)], fmm)

                def fgate(v):
                    lg = pb[4][:, 0:8]
                    v.reduce_max(out=c0[:], in_=lg, axis=AX.X)
                    v.drain()
                    v.tensor_scalar(out=t4[0][:, 0:8], in0=lg, scalar1=c0[:],
                                    scalar2=None, op0=ALU.is_ge)
                    v.drain()
                    v.scalar_tensor_tensor(out=t4[1][:, 0:8], in0=t4[0][:, 0:8],
                                           scalar=-1e30, in1=lg, op0=ALU.mult,
                                           op1=ALU.add)
                    v.drain()
                    v.reduce_max(out=c1[:], in_=t4[1][:, 0:8], axis=AX.X)
                    v.drain()
                    v.tensor_scalar(out=t4[2][:, 0:8], in0=lg, scalar1=c1[:],
                                    scalar2=None, op0=ALU.is_ge)
                    v.drain()
                    v.tensor_tensor(out=tmpc[:], in0=c1[:], in1=c0[:],
                                    op=ALU.subtract)
                    v.drain()
                    v.tensor_scalar(out=nmu[:], in0=c0[:], scalar1=-1.0,
                                    scalar2=None, op0=ALU.mult)
                mv = step("vector", [(T, mt)], fgate)

                def fexp(a):
                    a.activation(out=t4[3][:, 0:8], in_=pb[4][:, 0:8], func=AF.Exp,
                                 bias=nmu[:], scale=1.0)
                    a.drain()
                    a.activation(out=rstd[:], in_=tmpc[:], func=AF.Exp)
                ma = step("scalar", [(V, mv)], fexp)

                def fw(v, tt=tt):
                    v.tensor_scalar(out=rstd[:], in0=rstd[:], scalar1=1.0,
                                    scalar2=None, op0=ALU.add)
                    v.drain()
                    v.reciprocal(out=tmpc[:], in_=rstd[:])
                    v.drain()
                    v.tensor_scalar(out=t4[3][:, 0:8], in0=t4[3][:, 0:8],
                                    scalar1=tmpc[:], scalar2=None, op0=ALU.mult)
                    v.drain()
                    v.tensor_tensor(out=wtok[tt][:, 0:8], in0=t4[3][:, 0:8],
                                    in1=t4[2][:, 0:8], op=ALU.mult)
                step("vector", [(A, ma)], fw)
            mvz = step("vector", [(LI, LI.n)], lambda v: v.memset(evst[:], 0.0))
            for b in range(2):
                def ftr(t, b=b):
                    for i in range(4):
                        tt = 4 * b + i
                        t.transpose(out=ptr[:, 128 * i:128 * (i + 1)],
                                    in_=wtok[tt][:], identity=ident[:])
                mt = step("tensor", [(V, V.n)], ftr)

                def fev(v, b=b):
                    for i in range(4):
                        tt = 4 * b + i
                        v.tensor_copy(out=evst[0:8, 128 * tt:128 * (tt + 1)],
                                      in_=ptr[0:8, 128 * i:128 * (i + 1)])
                mvz = step("vector", [(T, mt)], fev)
            for tt in range(8):
                def fxa(t, tt=tt):
                    for k in range(16):
                        t.matmul(out=pb[4][:, 0:128],
                                 lhsT=wds[:, 2048 + 128 * k:2048 + 128 * (k + 1)],
                                 rhs=xn[:, NT * k + 128 * tt:NT * k + 128 * (tt + 1)],
                                 start=(k == 0), stop=(k == 15))
                mt = step("tensor", [(LW, mla), (V, m_xn1), (V, V.n)], fxa)
                mt2 = step("tensor", [(V, mvz), (LI, m_ident)],
                           lambda t, tt=tt: t.matmul(
                               out=pb[5][:, 0:128], lhsT=sel8[:],
                               rhs=evst[:, 128 * tt:128 * (tt + 1)],
                               start=True, stop=True))
                mv = step("vector", [(T, mt2)],
                          lambda v: v.tensor_copy(out=rkst[:, 0:128],
                                                  in_=pb[5][:, 0:128]))
                step("vector", [(V, mv), (T, mt)],
                     lambda v, tt=tt: v.tensor_tensor(
                         out=sqb[:, 128 * tt:128 * (tt + 1)],
                         in0=pb[4][:, 0:128], in1=rkst[:, 0:128], op=ALU.mult))
            m_xa = V.n
            mE = V.n
            for j in range(16):
                def flu(t, j=j):
                    for grp in range(2):
                        t.matmul(out=pb[2 + grp][:],
                                 lhsT=wds[:, 4096 + 128 * j:4096 + 128 * (j + 1)],
                                 rhs=sqb[:, 512 * grp:512 * (grp + 1)],
                                 start=True, stop=True)
                mt = step("tensor", [(LW, mlb2), (V, m_xa), (V, V.n)], flu)

                def fad(v, j=j):
                    import os as _o
                    mode = _o.environ.get("KFAD", "add")
                    for grp in range(2):
                        sl = dff[:, NT * j + 512 * grp:NT * j + 512 * (grp + 1)]
                        if mode == "copy":
                            v.tensor_copy(out=sl, in_=pb[2 + grp][:])
                        elif mode == "skip":
                            pass
                        else:
                            v.tensor_tensor(out=sl, in0=sl, in1=pb[2 + grp][:],
                                            op=ALU.add)
                mE = step("vector", [(T, mt), (V, m_base), (LI, LI.n)], fad)
            return mE

        m_dff1 = l1_gate_lora()
        dump_sb("d_dff1", dff[:], [(V, m_dff1)])
        if STOP <= 5:
            finish_early()
            return nc

        def l1_mix():
            mctx = V.n
            for h in range(H1):
                pairs = []
                for r in range(2):
                    for i4 in range(4):
                        g = 2 * i4 + r
                        pairs.append((khb[:, 256 * g:256 * (g + 1)],
                                      kvg1[16 * r + h, 0:128, 256 * i4:256 * (i4 + 1)]))
                        pairs.append((vhb[:, 256 * g:256 * (g + 1)],
                                      kvg1[16 * r + h, 128:256, 256 * i4:256 * (i4 + 1)]))
                mlk = dma("sync", [(V, V.n), (T, T.n), (G, m_cc1), (A, A.n),
                                   (LI, m_kv1)],
                          pairs + [(qhb[:], qdr1[h])])
                mvV = V.n
                for b in range(2):
                    def ftr(t, b=b):
                        for i in range(8):
                            kt = 8 * b + i
                            t.transpose(out=ptr[:, 128 * i:128 * (i + 1)],
                                        in_=vhb[:, 128 * kt:128 * (kt + 1)],
                                        identity=ident[:])
                    mt = step("tensor", [(LI, mlk), (V, mvV)], ftr)

                    def fevV(v, b=b):
                        for i in range(8):
                            kt = 8 * b + i
                            v.tensor_copy(out=vaug[:, 128 * kt:128 * (kt + 1)],
                                          in_=ptr[:, 128 * i:128 * (i + 1)])
                    mvV = step("vector", [(T, mt)], fevV)
                maQ = step("scalar", [(LI, mlk), (T, T.n), (V, mvV)],
                           lambda a: a.activation(out=sqb[:, 0:1024], in_=qhb[:],
                                                  func=AF.Square))
                mvz = V.n
                for grp in range(2):
                    mtq = step("tensor", [(A, maQ), (V, mvz)],
                               lambda t, grp=grp: t.matmul(
                                   out=pb[4][:], lhsT=ones_bf[:],
                                   rhs=sqb[:, 512 * grp:512 * (grp + 1)],
                                   start=True, stop=True))
                    mvz = step("vector", [(T, mtq)],
                               lambda v, grp=grp: v.tensor_scalar(
                                   out=zrow[0:1, 512 * grp:512 * (grp + 1)],
                                   in0=pb[4][0:1, :],
                                   scalar1=float(1.0 / 128.0), scalar2=EPS,
                                   op0=ALU.mult, op1=ALU.add))
                maK = step("scalar", [(LI, mlk), (T, T.n), (V, mvV)],
                           lambda a: a.activation(out=sqb[:], in_=khb[:],
                                                  func=AF.Square))
                mrk = V.n
                for g in range(4):
                    mtS = step("tensor", [(LI, mlk), (V, mrk)],
                               lambda t, g=g: t.matmul(out=pb[4][:], lhsT=ones_bf[:],
                                                       rhs=khb[:, 512 * g:512 * (g + 1)],
                                                       start=True, stop=True))
                    mtS2 = step("tensor", [(A, maK)],
                                lambda t, g=g: t.matmul(out=pb[5][:], lhsT=ones_bf[:],
                                                        rhs=sqb[:, 512 * g:512 * (g + 1)],
                                                        start=True, stop=True))

                    def fvar(v):
                        v.tensor_scalar(out=muq, in0=pb[4][:],
                                        scalar1=float(1.0 / 128.0), scalar2=None,
                                        op0=ALU.mult)
                        v.drain()
                        v.tensor_tensor(out=muq, in0=muq, in1=muq,
                                        op=ALU.mult)
                        v.drain()
                        v.scalar_tensor_tensor(out=muq, in0=pb[5][:],
                                               scalar=float(1.0 / 128.0), in1=muq,
                                               op0=ALU.mult, op1=ALU.subtract)
                        v.drain()
                        v.tensor_scalar(out=muq, in0=muq, scalar1=EPS,
                                        scalar2=None, op0=ALU.add)
                        v.drain()
                        v.reciprocal(out=muq, in_=muq)
                    mv = step("vector", [(T, mtS), (T, mtS2)], fvar)
                    ma2 = step("scalar", [(V, mv)],
                               lambda a: a.activation(out=muq, in_=muq,
                                                      func=AF.Sqrt))
                    mv2 = step("vector", [(A, ma2)],
                               lambda v: v.tensor_scalar(
                                   out=rkst[0:1, :], in0=muq[0:1, 0:512],
                                   scalar1=float(1.0 / np.sqrt(128.0)),
                                   scalar2=None, op0=ALU.mult))

                    def ftr2(t):
                        for i in range(4):
                            t.transpose(out=ptr[:, 128 * i:128 * (i + 1)],
                                        in_=rkst[:, 128 * i:128 * (i + 1)],
                                        identity=ident[:])
                    mt3 = step("tensor", [(V, mv2)], ftr2)

                    def fev3(v, g=g):
                        for i in range(4):
                            v.tensor_copy(out=rk[:, 4 * g + i:4 * g + i + 1],
                                          in_=ptr[:, 128 * i:128 * i + 1])
                    mrk = step("vector", [(T, mt3)], fev3)
                for qi_ in range(4):
                    ntr = 4 * qi_ + 4
                    maE = A.n
                    for kt in range(ntr):
                        bank = kt % 2
                        mt = step("tensor", [(LI, mlk), (V, mrk), (A, maE)],
                                  lambda t, kt=kt, qi_=qi_, bank=bank: t.matmul(
                                      out=pb[bank][:, 0:256],
                                      lhsT=khb[:, 128 * kt:128 * (kt + 1)],
                                      rhs=qhb[:, 256 * qi_:256 * (qi_ + 1)],
                                      start=True, stop=True))
                        maE = step("scalar", [(T, mt), (V, mrk)],
                                   lambda a, kt=kt, bank=bank: a.activation(
                                       out=exps[:, 256 * kt:256 * (kt + 1)],
                                       in_=pb[bank][:, 0:256], func=AF.Square,
                                       scale=rk[:, kt:kt + 1]))

                    def fmask(v, qi_=qi_):
                        for r in range(4):
                            kt = 4 * qi_ + r
                            v.tensor_tensor(out=exps[:, 256 * kt:256 * (kt + 1)],
                                            in0=exps[:, 256 * kt:256 * (kt + 1)],
                                            in1=masks[:, 256 * r:256 * (r + 1)],
                                            op=ALU.mult)
                            v.drain()
                    mv = step("vector", [(A, maE), (LI, m_ident)], fmask)

                    def fpv(t, ntr=ntr):
                        for kt in range(ntr):
                            t.matmul(out=pb[2][:, 0:256],
                                     lhsT=vaug[:, 128 * kt:128 * (kt + 1)],
                                     rhs=exps[:, 256 * kt:256 * (kt + 1)],
                                     start=(kt == 0), stop=(kt == ntr - 1))
                            t.matmul(out=pb[3][:, 0:256], lhsT=ones_bf[:],
                                     rhs=exps[:, 256 * kt:256 * (kt + 1)],
                                     start=(kt == 0), stop=(kt == ntr - 1))
                    mt = step("tensor", [(V, mv), (V, mvV)], fpv)

                    def fno(v, qi_=qi_):
                        v.scalar_tensor_tensor(out=muq[0:1, 0:256],
                                               in0=zrow[0:1, 256 * qi_:256 * (qi_ + 1)],
                                               scalar=EPS, in1=pb[3][0:1, 0:256],
                                               op0=ALU.mult, op1=ALU.add)
                        v.drain()
                        v.reciprocal(out=muq[0:1, 0:256], in_=muq[0:1, 0:256])
                        v.drain()
                        v.tensor_copy(out=st0[0:1, 0:256], in_=muq[0:1, 0:256])
                    mv2 = step("vector", [(T, mt)], fno)
                    mt2 = step("tensor", [(V, mv2)],
                               lambda t: t.matmul(out=pb[4][:, 0:256],
                                                  lhsT=ones_bf[:],
                                                  rhs=st0[:, 0:256],
                                                  start=True, stop=True))
                    mv3 = step("vector", [(T, mt2)],
                               lambda v: v.tensor_copy(out=rkst[:, 0:256],
                                                       in_=pb[4][:, 0:256]))

                    def fctx(v, h=h, qi_=qi_):
                        import os as _o
                        sl2 = ctxb[:, NT * h + 256 * qi_:NT * h + 256 * (qi_ + 1)]
                        m2 = _o.environ.get("KCTX", "mult")
                        if m2 == "copy":
                            v.tensor_copy(out=sl2, in_=pb[2][:, 0:256])
                        elif m2 == "div":
                            v.tensor_copy(out=sl2, in_=rkst[:, 0:256])
                        else:
                            v.tensor_tensor(out=sl2, in0=pb[2][:, 0:256],
                                            in1=rkst[:, 0:256], op=ALU.mult)
                    mctx = step("vector", [(V, mv3)], fctx)
            return mctx

        m_ctx1 = l1_mix()
        dump_sb("d_ctx1", ctxb, [(V, m_ctx1)])
        dump_sb("d_rk", rk[:], [(V, m_ctx1)])
        dump_sb("d_zr", zrow[0:1, :], [(V, m_ctx1)])
        dump_sb("d_att", exps[:, 0:4096], [(V, m_ctx1)])
        dump_sb("d_div", rkst[:, 0:256], [(V, m_ctx1)])
        dump_sb("d_vaug", vaug, [(V, m_ctx1)])
        dump_sb("d_khb", khb, [(V, m_ctx1)])

        def l1_out():
            mE = V.n
            for j in range(16):
                mw = dma1("gpsimd", [(T, T.n)], wgs[:, 0:2048], w1_in[3][j])

                def fmm(t, j=j):
                    for grp in range(2):
                        for k in range(16):
                            t.matmul(out=pb[grp][:],
                                     lhsT=wgs[:, 128 * k:128 * (k + 1)],
                                     rhs=ctxb[:, NT * k + 512 * grp:NT * k + 512 * (grp + 1)],
                                     start=(k == 0), stop=(k == 15))
                mt = step("tensor", [(LW, mw), (V, m_ctx1), (A, A.n), (V, V.n)], fmm)

                def fev(v, j=j):
                    for grp in range(2):
                        v.tensor_tensor(
                            out=dff[:, NT * j + 512 * grp:NT * j + 512 * (grp + 1)],
                            in0=dff[:, NT * j + 512 * grp:NT * j + 512 * (grp + 1)],
                            in1=pb[grp][:], op=ALU.add)
                mE = step("vector", [(T, mt), (V, m_dff1), (LI, LI.n)], fev)
            mo = 0
            for i in range(8):
                mlx = dma1("sync", [(V, V.n), (A, A.n), (LI, m_x1)], xt[:],
                           x1own[128 * i:128 * (i + 1), :])
                mE2 = V.n
                for b in range(2):
                    def ftr(t, i=i, b=b):
                        for j in range(8):
                            k = 8 * b + j
                            t.transpose(out=ptr[:, 128 * j:128 * (j + 1)],
                                        in_=dff[:, NT * k + 128 * i:NT * k + 128 * (i + 1)],
                                        identity=ident[:])
                    mt = step("tensor", [(V, mE), (V, V.n)], ftr)

                    def fevd(v, b=b):
                        for j in range(8):
                            k = 8 * b + j
                            v.tensor_copy(out=dtm[:, 128 * k:128 * (k + 1)],
                                          in_=ptr[:, 128 * j:128 * (j + 1)])
                    mE2 = step("vector", [(T, mt)], fevd)
                mv = step("vector", [(V, mE2), (LI, mlx)],
                          lambda v: v.tensor_tensor(out=xt[:], in0=xt[:],
                                                    in1=dtm, op=ALU.add))
                mo = dma1("sync", [(V, mv)], yo[128 * i:128 * (i + 1), :], xt[:])

            def ffin(s):
                s.wait_ge(LI.h, mo)
            blk.sync(ffin)
            return mo

        l1_out()

    return nc


def _tile_wT(w_t):
    out = np.empty((16, 128, 2048), np.float32)
    for j in range(16):
        for k in range(16):
            out[j][:, 128 * k:128 * (k + 1)] = w_t[128 * k:128 * (k + 1),
                                                   128 * j:128 * (j + 1)]
    return out


_NC_CACHE = {}


def kernel(**inputs):
    ii = {k: np.asarray(v) for k, v in inputs.items()}
    x = ii["L0_attn_hidden_states"].astype(np.float32)

    wA = [np.ascontiguousarray(ii[f"L0_attn_{n}_weight"].reshape(16, 128, D))
          .astype(np.float32) for n in ("q", "k", "v", "o")]
    wB = []
    for n in ("q", "k", "v", "o"):
        wt = ii[f"L0_attn_{n}_weight"].T
        t = np.empty((16, 4, 128, 512), np.float32)
        for k in range(16):
            for g in range(4):
                t[k, g] = wt[128 * k:128 * (k + 1), 512 * g:512 * (g + 1)]
        wB.append(t)
    l0b = [ii[f"L0_attn_{n}_bias"].astype(np.float32) for n in ("q", "k", "v", "o")]

    idx = np.empty(8192, np.int64)
    pos = 0
    for sl in range(16):
        for r in range(4):
            for i in range(128):
                idx[pos] = 512 * sl + 4 * i + r
                pos += 1
    wg = ii["L0_ffn_w_gate"][:, idx]
    wu = ii["L0_ffn_w_up"][:, idx]
    wd = ii["L0_ffn_w_down"][idx, :]

    wg_t = np.ascontiguousarray(
        wg.reshape(16, 128, 16, 512).transpose(2, 1, 0, 3).reshape(16, 128, 8192)
    ).astype(np.float32)
    wu_t = np.ascontiguousarray(
        wu.reshape(16, 128, 16, 512).transpose(2, 1, 0, 3).reshape(16, 128, 8192)
    ).astype(np.float32)
    wd_t = np.ascontiguousarray(
        wd.reshape(64, 128, 16, 128).transpose(2, 1, 0, 3).reshape(16, 128, 8192)
    ).astype(np.float32)

    w1 = [_tile_wT(ii[f"L1_attn_W{n}"].T) for n in ("q", "k", "v", "o")]
    wf1 = _tile_wT(ii["L1_ffn_W"].T)
    laT = ii["L1_ffn_lora_A"].reshape(128, D).T
    laT_t = np.ascontiguousarray(
        laT.reshape(16, 128, 128).transpose(1, 0, 2).reshape(128, 2048)
    ).astype(np.float32)
    lb = ii["L1_ffn_lora_B"]
    Bst = np.zeros((128, D), np.float32)
    for e in range(8):
        for r2 in range(16):
            Bst[16 * e + r2] = lb[e, :, r2]
    lbS_t = np.ascontiguousarray(Bst)
    gwT_full = np.zeros((D, 128), np.float32)
    gwT_full[:, 0:8] = ii["L1_ffn_gate_w"].T
    gwT_t = np.ascontiguousarray(
        gwT_full.reshape(16, 128, 128).transpose(1, 0, 2).reshape(128, 2048))

    ident = np.eye(128, dtype=ml_dtypes.bfloat16)
    sel8np = np.zeros((128, 128), np.float32)
    for e in range(8):
        sel8np[e, 16 * e:16 * (e + 1)] = 1.0
    sel8np = sel8np.astype(ml_dtypes.bfloat16)

    if "nc" not in _NC_CACHE:
        _NC_CACHE["nc"] = build_nc()
    nc = _NC_CACHE["nc"]

    in_maps = []
    for c in range(8):
        s, par = c // 2, c % 2
        gids = [2 * i + par for i in range(4)]
        xown = np.ascontiguousarray(
            np.concatenate([x[s, 256 * g:256 * (g + 1), :] for g in gids], axis=0))
        kk = np.arange(128)[:, None]
        qq = np.arange(256)[None, :]
        mask1 = np.stack([(128 * r + kk <= 256 * par + qq) for r in range(4)]) \
            .astype(ml_dtypes.bfloat16)
        im = {"xo": xown, "wg_t": wg_t, "wu_t": wu_t, "wd_t": wd_t,
              "gam": ii["L1_attn_gamma"].astype(np.float32),
              "bet": ii["L1_attn_beta"].astype(np.float32),
              "wf1": wf1, "b1": ii["L1_ffn_b"].astype(np.float32),
              "laT": laT_t, "lbS": lbS_t, "gwT": gwT_t,
              "gb": ii["L1_ffn_gate_b"].astype(np.float32),
              "ident": ident, "mask1": mask1, "sel8": sel8np}
        for i in range(4):
            im[f"l0wA{i}"] = wA[i]
            im[f"l0wB{i}"] = wB[i]
            im[f"l0b{i}"] = l0b[i]
            im[f"l1w{i}"] = w1[i]
        in_maps.append(im)

    r = run_bass_kernel_spmd(nc, in_maps, core_ids=list(range(8)))

    out = np.empty((4, 2048, D), np.float32)
    for c in range(8):
        s, par = c // 2, c % 2
        y = r.results[c]["yo"]
        for i, g in enumerate([2 * i + par for i in range(4)]):
            out[s, 256 * g:256 * (g + 1), :] = y[256 * i:256 * (i + 1), :]
    return out


# version 13 (final)
# speedup vs baseline: 1.0332x; 1.0332x over previous
import os
import numpy as np
import ml_dtypes
from contextlib import ExitStack

import concourse.bass as bass
import concourse.mybir as mybir
from concourse.bass_utils import run_bass_kernel_spmd

F32 = mybir.dt.float32
BF16 = mybir.dt.bfloat16
I32 = mybir.dt.int32
AF = mybir.ActivationFunctionType
ALU = mybir.AluOpType
AX = mybir.AxisListType

D = 2048
NT = 1024
H0, DH0 = 32, 64
H1, DH1 = 16, 128
NCH = 4
EPS = 1e-5


class Ch:
    def __init__(self, stack, nc, name, step=1):
        self.h = stack.enter_context(nc.semaphore(name))
        self.n = 0
        self.step = step


def build_nc():
    nc = bass.Bass("TRN2", target_bir_lowering=False)

    def din(name, shape, dt=F32):
        return nc.dram_tensor(name, shape, dt, kind="ExternalInput").ap()

    xo = din("xo", [NT, D])
    wA = [din(f"l0wA{i}", [16, 128, D]) for i in range(4)]
    wB = [din(f"l0wB{i}", [16, 4, 128, 512]) for i in range(4)]
    l0b = [din(f"l0b{i}", [D]) for i in range(4)]
    wg_in = din("wg_t", [16, 128, 16 * 512])
    wu_in = din("wu_t", [16, 128, 16 * 512])
    wd_in = din("wd_t", [16, 128, 64 * 128])
    w1_in = [din(f"l1w{i}", [16, 128, D]) for i in range(4)]
    gam = din("gam", [128])
    bet = din("bet", [128])
    wf1 = din("wf1", [16, 128, D])
    b1v = din("b1", [D])
    laT_in = din("laT", [128, 2048])
    lbS_in = din("lbS", [128, 2048])
    gwT_in = din("gwT", [128, 2048])
    gbv = din("gb", [8])
    ident_in = din("ident", [128, 128], BF16)
    mask_in = din("mask1", [4, 128, 256], BF16)
    sel8_in = din("sel8", [128, 128], BF16)

    yo = nc.dram_tensor("yo", [NT, D], F32, kind="ExternalOutput").ap()
    KDUMP = os.environ.get("KDUMP", "0") == "1"
    dmp = {}
    if KDUMP:
        for nm, shape, dt in [("d_xn", [128, 16 * NT], BF16),
                              ("d_res", [128, 16 * NT], BF16),
                              ("d_kvs", [H0, 128, NT], BF16),
                              ("d_qdr", [H0, DH0, NT], BF16),
                              ("d_ctx", [128, 16 * NT], BF16),
                              ("d_dff", [128, 16 * NT], BF16),
                              ("d_x1", [NT, D], F32),
                              ("d_kvs1", [H1, 256, NT], BF16),
                              ("d_qdr1", [H1, DH1, NT], BF16),
                              ("d_ctx1", [128, 16 * NT], BF16),
                              ("d_dff1", [128, 16 * NT], BF16),
                              ("d_dffB", [128, 16 * NT], BF16),
                              ("d_rk", [128, 16], F32),
                              ("d_div", [128, 256], BF16),
                              ("d_vaug", [128, 2048], BF16),
                              ("d_khb", [128, 2048], BF16),
                              ("d_zr", [1, NT], F32),
                              ("d_att", [128, 16 * 256], BF16),
                              ("d_cb", [128, 64], F32),
                              ("d_cmax", [128, 64], F32),
                              ("d_s", [128, 64], F32),
                              ("d_idx", [128, 64], F32),
                              ("d_tmax", [128, 4], F32)]:
            dmp[nm] = nc.dram_tensor(nm, shape, dt, kind="ExternalOutput").ap()

    kvs = nc.dram_tensor("kvs", [H0, 128, NT], BF16).ap()
    kvg = nc.dram_tensor("kvg", [2 * H0, 128, NT], BF16).ap()
    kvs1 = nc.dram_tensor("kvs1", [H1, 256, NT], BF16).ap()
    kvg1 = nc.dram_tensor("kvg1", [2 * H1, 256, NT], BF16).ap()
    qdr = nc.dram_tensor("qdr", [H0, DH0, NT], BF16).ap()
    qdr1 = nc.dram_tensor("qdr1", [H1, DH1, NT], BF16).ap()
    x1own = nc.dram_tensor("x1own", [NT, D], F32).ap()

    with ExitStack() as stack:
        def sb(name, shape, dt):
            return stack.enter_context(nc.sbuf_tensor(name, shape, dt))

        LI = Ch(stack, nc, "LI", 16)
        LW = Ch(stack, nc, "LW", 16)
        T = Ch(stack, nc, "T")
        V = Ch(stack, nc, "V")
        A = Ch(stack, nc, "A")
        G = Ch(stack, nc, "G")
        blk = stack.enter_context(nc.Block())
        seen = {e: {} for e in ("tensor", "vector", "scalar", "sync", "gpsimd")}
        OWN = {"tensor": T, "vector": V, "scalar": A, "sync": LI, "gpsimd": G}

        def step(engname, deps, fn, raw=False):
            own = OWN[engname]

            def body(eng):
                sn = seen[engname]
                alldeps = list(deps) + [(own, own.n)]
                for c, m in alldeps:
                    if m > 0 and m > sn.get(id(c), 0):
                        eng.wait_ge(c.h, m)
                        sn[id(c)] = m
                r = fn(eng)
                if raw:
                    r.then_inc(own.h, 1)
                else:
                    eng.drain().then_inc(own.h, 1)
                own.n += 1
            getattr(blk, engname)(body)
            return own.n

        def dma(engname, deps, pairs):
            ch = LI if engname == "sync" else LW

            def body(eng):
                sn = seen[engname]
                for c, m in deps:
                    if m > 0 and m > sn.get(id(c), 0):
                        eng.wait_ge(c.h, m)
                        sn[id(c)] = m
                for o, i in pairs:
                    eng.dma_start(out=o, in_=i).then_inc(ch.h, 16)
                    ch.n += 16
            getattr(blk, engname)(body)
            return ch.n

        def dma1(engname, deps, o, i):
            return dma(engname, deps, [(o, i)])

        pb = [stack.enter_context(nc.psum_tensor(f"pb{i}", [128, 512], F32))
              for i in range(6)]
        ptr = stack.enter_context(nc.psum_tensor("ptr", [128, 1024], BF16))

        MINSB = os.environ.get("MINSB", "0") == "1"
        bigA = sb("bigA", [128, 2048 if MINSB else 32768], BF16)
        if MINSB:
            res = ctxb = hspI = exps = khb = vhb = vaug = qhb = dtm = xnt = bigA
            sqb = vhb
        else:
            res = bigA[:, 0:16384]
            ctxb = bigA[:, 16384:32768]
            hspI = bigA[:, 0:32768]
            exps = bigA[:, 0:8192]
            khb = bigA[:, 8192:10240]
            vhb = bigA[:, 10240:12288]
            sqb = vhb
            vaug = bigA[:, 12288:14336]
            qhb = bigA[:, 14336:15360]
            dtm = bigA[:, 16384:18432]
            xnt = bigA[:, 16384:18432]

        ones_bf = sb("ones_bf", [128, 128], BF16)
        ident = sb("identsb", [128, 128], BF16)
        sel8 = sb("sel8sb", [128, 128], BF16)
        st0 = sb("st0", [128, 512], BF16)
        st0r = sb("st0r", [128, 128], BF16)
        gcol = sb("gcol", [128, 1], F32)
        bcol = sb("bcol", [128, 1], F32)
        masks = sb("masks", [128, 4 * 256], BF16)
        b1c = sb("b1c", [128, 16], F32)

        cbf32 = sb("cbf32", [128, 64], F32)
        cbF = sb("cbF", [128, 64], BF16)
        cbQ = sb("cbQ", [128, 64], F32)
        aQ = sb("aQ", [128, 64], F32)
        sQ = sb("sQ", [128, 64], F32)
        cmaxQ = sb("cmaxQ", [128, 64], F32)
        rmaxQ = sb("rmaxQ", [128, 1], F32)
        rminQ = sb("rminQ", [128, 1], F32)
        tmaxQ = sb("tmaxQ", [128, NCH], F32)
        idxQ = sb("idxQ", [128, 64], F32)
        accQ = sb("accQ", [128, 64], F32)
        thrQ = sb("thrQ", [128, 1], F32)
        tmxs = sb("tmxs", [128, 128], BF16)
        trow = sb("trow", [128, 128], BF16)
        trowf = sb("trowf", [1, 128], F32)
        qu = sb("qu", [128, 512], F32)
        qi = sb("qi", [128, 512], I32)
        qf = sb("qf", [128, 512], F32)

        wa_sb = sb("wa_sb", [128, 2048], F32)
        wdq = sb("wdq", [128, 2048], BF16)
        wsc = [sb(f"wsc{i}", [128, 1], F32) for i in range(4)]
        wdqB = sb("wdqB", [128, 16 * 128], BF16)
        wbo = sb("wbo", [128, 512], BF16)
        cbst = sb("cbst", [128, 128], BF16)
        biasb = sb("biasb", [128, 512], BF16)
        biasf = sb("biasf", [128, 64], F32)

        evst = sb("evst", [128, 1024], BF16)

        xn = sb("xn", [128, 128 if MINSB else 16 * NT], BF16)
        dff = sb("dff", [128, 128 if MINSB else 16 * NT], BF16)

        xt = sb("xt", [128, D], F32)
        bnst = sb("bnst", [128, 24], F32)
        bnag = sb("bnag", [128, 2], F32)
        rstd = sb("rstd", [128, 1], F32)
        nmu = sb("nmu", [128, 1], F32)
        tmpc = sb("tmpc", [128, 1], F32)

        wgs = sb("wgs", [128, 16 * 512], BF16)
        wus = sb("wus", [128, 16 * 512], BF16)
        wds = wus
        hh2 = [sb(f"h2b{i}", [128, 512], BF16) for i in range(2)]
        s2 = sb("s2", [128, 512], BF16)
        t4 = [evst[:, 512 + 128 * i:512 + 128 * (i + 1)] for i in range(4)]

        nbias = sb("nbias", [128, 1], F32)
        c85 = sb("c85", [128, 1], F32)
        c0 = sb("c0", [128, 1], F32)
        c1 = sb("c1", [128, 1], F32)
        rowb = wa_sb[0:1, 0:1024]
        srow = wa_sb[0:1, 0:512]
        zrow = rowb
        rk = sb("rk", [128, 16], F32)
        rkst = s2
        muq = qu[:, 0:512]
        wtok = [wdq[:, 128 * i:128 * (i + 1)] for i in range(8)]


        KSK0 = os.environ.get("KSKIP", "")
        if "identdma" in KSK0:
            m_ident = 0
        elif "maskdma" in KSK0:
            m_ident = dma("sync", [], [(ident[:], ident_in), (sel8[:], sel8_in)])
        else:
            m_ident = dma("sync", [], [
                (ident[:], ident_in),
                (sel8[:], sel8_in)]
                + [(masks[:, 256 * w:256 * (w + 1)], mask_in[w]) for w in range(4)])

        def finit(v):
            v.memset(ones_bf[:], 1.0)
            v.memset(st0[:], 0.0)
            v.memset(st0r[:], 0.0)
            v.memset(tmxs[:], 0.0)
            v.memset(c85[:], 8.5)
            for w in wtok:
                v.memset(w[:], 0.0)
        m_init = 0 if "finit" in KSK0 else step("vector", [], finit)

        KSKIP0 = os.environ.get("KSKIP", "")
        if "gb" in KSKIP0:
            m_gb = 0
        else:
            with nc.allow_non_contiguous_dma(reason="one-time [128,1] vector loads"):
                m_gb = dma("sync", [], [
                    (gcol[:], gam.rearrange("(p o) -> p o", o=1)),
                    (bcol[:], bet.rearrange("(p o) -> p o", o=1))])
        m_b1c = 0
        mtb = T.n
        for jj in ([] if "b1c" in os.environ.get("KSKIP", "") else range(16)):
            mlj = dma1("gpsimd", [(V, m_init), (T, mtb)], st0r[0:1, 0:128],
                       b1v.rearrange("(o n) -> o n", o=1)[0:1, 128 * jj:128 * (jj + 1)])
            mtb = step("tensor", [(LW, mlj), (LI, m_ident)],
                       lambda t: t.transpose(out=ptr[:, 0:128], in_=st0r[:],
                                             identity=ident[:]))
            m_b1c = step("vector", [(T, mtb)],
                         lambda v, jj=jj: v.tensor_copy(out=b1c[:, jj:jj + 1],
                                                        in_=ptr[:, 0:1]))
        m_init2 = 0 if "finit" in KSK0 else step("vector", [], lambda v: v.memset(st0r[:], 0.0))

        def ln_to_xn(src_dram, pre_deps):
            mv = V.n
            ma2 = A.n
            for i in range(8):
                ml = dma1("sync", [(V, mv), (A, ma2)] + pre_deps,
                          xt[:], src_dram[128 * i:128 * (i + 1), :])

                def fstats(v):
                    for j in range(4):
                        v.bn_stats(out=bnst[:, 6 * j:6 * j + 6],
                                   in_=xt[:, 512 * j:512 * (j + 1)])
                    v.drain()
                    v.bn_aggr(out=bnag[:], in_=bnst[:])
                    v.drain()
                    v.tensor_scalar(out=tmpc[:], in0=bnag[:, 1:2], scalar1=EPS,
                                    scalar2=None, op0=ALU.add)
                    v.drain()
                    v.reciprocal(out=rstd[:], in_=tmpc[:])
                mv1 = step("vector", [(LI, ml)], fstats)
                ma1 = step("scalar", [(V, mv1)],
                           lambda a: a.activation(out=rstd[:], in_=rstd[:],
                                                  func=AF.Sqrt))
                mv2 = step("vector", [(A, ma1)],
                           lambda v: v.tensor_scalar(out=nmu[:], in0=bnag[:, 0:1],
                                                     scalar1=rstd[:], scalar2=-1.0,
                                                     op0=ALU.mult, op1=ALU.mult))
                ma2 = step("scalar", [(V, mv2), (T, T.n), (V, V.n)],
                           lambda a: a.activation(out=xnt, in_=xt[:],
                                                  func=AF.Identity,
                                                  scale=rstd[:], bias=nmu[:]))
                for b in range(2):
                    def ftr(t, b=b):
                        for j in range(8):
                            k = 8 * b + j
                            t.transpose(out=ptr[:, 128 * j:128 * (j + 1)],
                                        in_=xnt[:, 128 * k:128 * (k + 1)],
                                        identity=ident[:])
                    mt = step("tensor", [(A, ma2), (V, V.n), (LI, m_ident)], ftr)

                    def fev(v, b=b, i=i):
                        for j in range(8):
                            k = 8 * b + j
                            v.tensor_copy(
                                out=xn[:, NT * k + 128 * i:NT * k + 128 * (i + 1)],
                                in_=ptr[:, 128 * j:128 * (j + 1)])
                    mv = step("vector", [(T, mt)], fev)
            return mv

        def quantize(src, dst):
            mstat = step("vector", [], lambda v: v.memset(tmxs[:], 0.0))
            for k in range(16):
                def fst(v, k=k):
                    for c in range(NCH):
                        sl = src[:, NT * k + 256 * c:NT * k + 256 * (c + 1)]
                        v.reduce_max(out=rmaxQ[:], in_=sl, axis=AX.X)
                        v.drain()
                        v.tensor_reduce(out=rminQ[:], in_=sl, axis=AX.X, op=ALU.min)
                        v.drain()
                        v.tensor_tensor(out=tmpc[:], in0=rmaxQ[:], in1=rminQ[:],
                                        op=ALU.add)
                        v.drain()
                        col = 4 * k + c
                        v.tensor_scalar(out=cbf32[:, col:col + 1], in0=tmpc[:],
                                        scalar1=0.5, scalar2=None, op0=ALU.mult)
                        v.drain()
                        v.tensor_copy(out=cbF[:, col:col + 1],
                                      in_=cbf32[:, col:col + 1])
                        v.drain()
                        v.tensor_scalar(out=qu[:, 256 * (c % 2):256 * (c % 2 + 1)],
                                        in0=sl, scalar1=cbf32[:, col:col + 1],
                                        scalar2=None, op0=ALU.subtract)
                        v.drain()
                        v.reduce_max(out=cmaxQ[:, col:col + 1],
                                     in_=qu[:, 256 * (c % 2):256 * (c % 2 + 1)],
                                     axis=AX.X, apply_absolute_value=True)
                        v.drain()
                mstat = step("vector", [], fst)
            mv = step("vector", [], lambda v: v.tensor_copy(out=tmxs[:, 0:64],
                                                            in_=cmaxQ[:]))
            mt = step("tensor", [(V, mv), (LI, m_ident), (V, V.n)],
                      lambda t: t.transpose(out=ptr[:, 0:128], in_=tmxs[:],
                                            identity=ident[:]))
            mv = step("vector", [(T, mt)],
                      lambda v: v.tensor_copy(out=trow[:], in_=ptr[:, 0:128]))
            mv = step("vector", [],
                      lambda v: v.tensor_reduce(out=tmxs[:, 0:1], in_=trow[:],
                                                axis=AX.X, op=ALU.max))
            mt = step("tensor", [(V, mv)],
                      lambda t: t.transpose(out=ptr[:, 0:128], in_=tmxs[:],
                                            identity=ident[:]))
            mv = step("vector", [(T, mt)],
                      lambda v: v.tensor_copy(out=trow[:], in_=ptr[:, 0:128]))

            def ftm2(v):
                g = trow[0:1, 0:64].rearrange("p (k o) -> p k o", o=4)
                for c in range(NCH):
                    v.tensor_reduce(out=trowf[:, c:c + 1], in_=g[:, :, c],
                                    axis=AX.X, op=ALU.max)
                v.drain()
                v.tensor_copy(out=st0r[0:1, 0:4], in_=trowf[:, 0:4])
            mv = step("vector", [], ftm2)
            mt = step("tensor", [(V, mv)],
                      lambda t: t.matmul(out=pb[5][:, 0:128], lhsT=ones_bf[:],
                                         rhs=st0r[:], start=True, stop=True))
            mv = step("vector", [(T, mt)],
                      lambda v: v.tensor_copy(out=tmaxQ[:], in_=pb[5][:, 0:4]))

            def fscale(v):
                v.memset(idxQ[:], 0.0)
                v.drain()
                cm = cmaxQ[:, 0:64].rearrange("p (k o) -> p k o", o=4)
                ix = idxQ[:, 0:64].rearrange("p (k o) -> p k o", o=4)
                for c in range(NCH):
                    for j in range(7):
                        v.tensor_scalar(out=thrQ[:], in0=tmaxQ[:, c:c + 1],
                                        scalar1=float(2.0 ** (j - 7)), scalar2=None,
                                        op0=ALU.mult)
                        v.drain()
                        v.tensor_scalar(out=accQ[:, 0:16], in0=cm[:, :, c],
                                        scalar1=thrQ[:], scalar2=None, op0=ALU.is_gt)
                        v.drain()
                        v.tensor_tensor(out=ix[:, :, c], in0=ix[:, :, c],
                                        in1=accQ[:, 0:16], op=ALU.add)
                        v.drain()
                v.memset(sQ[:], 0.0)
                v.drain()
                for jj in range(8):
                    v.scalar_tensor_tensor(out=accQ[:], in0=idxQ[:], scalar=float(jj),
                                           in1=sQ[:], op0=ALU.is_equal, op1=ALU.bypass)
                    v.drain()
                    v.tensor_scalar(out=accQ[:], in0=accQ[:],
                                    scalar1=float(2.0 ** (jj - 7)), scalar2=None,
                                    op0=ALU.mult)
                    v.drain()
                    v.tensor_tensor(out=sQ[:], in0=sQ[:], in1=accQ[:], op=ALU.add)
                    v.drain()
                sv = sQ[:, 0:64].rearrange("p (k o) -> p k o", o=4)
                for c in range(NCH):
                    v.tensor_scalar(out=sv[:, :, c], in0=sv[:, :, c],
                                    scalar1=tmaxQ[:, c:c + 1],
                                    scalar2=float(1.0 / 7.0),
                                    op0=ALU.mult, op1=ALU.mult)
                v.drain()
                v.tensor_scalar(out=sQ[:], in0=sQ[:], scalar1=1e-9, scalar2=None,
                                op0=ALU.max)
                v.drain()
                v.reciprocal(out=aQ[:], in_=sQ[:])
                v.drain()
                v.tensor_tensor(out=cbQ[:], in0=cbf32[:], in1=aQ[:], op=ALU.mult)
                v.drain()
                v.tensor_scalar(out=cbQ[:], in0=cbQ[:], scalar1=-1.0, scalar2=None,
                                op0=ALU.mult)
            mv = step("vector", [], fscale)

            mg = G.n
            mvf = mv
            for k in range(16):
                for cp in range(2):
                    def fu(v, k=k, cp=cp):
                        for c2 in range(2):
                            c = 2 * cp + c2
                            col = 4 * k + c
                            v.tensor_scalar(out=qu[:, 256 * c2:256 * (c2 + 1)],
                                            in0=src[:, NT * k + 256 * c:NT * k + 256 * (c + 1)],
                                            scalar1=aQ[:, col:col + 1],
                                            scalar2=cbQ[:, col:col + 1],
                                            op0=ALU.mult, op1=ALU.add)
                    mvk = step("vector", [], fu)

                    def fcvt(v):
                        v.tensor_copy(out=qi[:], in_=qu[:])
                        v.drain()
                        v.tensor_copy(out=qf[:], in_=qi[:])
                    mg2 = step("vector", [], fcvt)

                    def ff(v, k=k, cp=cp):
                        for c2 in range(2):
                            c = 2 * cp + c2
                            col = 4 * k + c
                            v.tensor_scalar(
                                out=dst[:, NT * k + 256 * c:NT * k + 256 * (c + 1)],
                                in0=qf[:, 256 * c2:256 * (c2 + 1)],
                                scalar1=sQ[:, col:col + 1], scalar2=None,
                                op0=ALU.mult)
                    mvf = step("vector", [], ff)
            return mvf

        def cb_bias_rows(wB_d, bias_d, scaleq):
            mE = V.n
            for g in range(4):
                mmm = T.n
                for k in range(16):
                    def fstage(v, k=k):
                        v.memset(cbst[:], 0.0)
                        v.drain()
                        v.tensor_copy(out=cbst[:, 0:4], in_=cbF[:, 4 * k:4 * k + 4])
                    mvS = step("vector", [(T, mmm)], fstage)
                    mgW = dma1("gpsimd", [(T, mmm)], wbo[:], wB_d[k, g])
                    mmm = step("tensor", [(V, mvS), (LW, mgW)],
                               lambda t, k=k: t.matmul(out=pb[4][:], lhsT=cbst[:],
                                                       rhs=wbo[:], start=(k == 0),
                                                       stop=False))
                mlb = dma1("gpsimd", [(T, mmm), (V, V.n)], st0[0:1, :],
                           bias_d.rearrange("(o n) -> o n", o=1)[0:1, 512 * g:512 * (g + 1)])
                mmm = step("tensor", [(LW, mlb)],
                           lambda t: t.matmul(out=pb[4][:], lhsT=ones_bf[:],
                                              rhs=st0[:], start=False, stop=True))
                mvE = step("vector", [(T, mmm)],
                           lambda v: v.tensor_copy(out=biasb[0:4, :],
                                                   in_=pb[4][0:4, :]))
                for i in range(4):
                    oc = 4 * g + i
                    mt = step("tensor", [(V, mvE)],
                              lambda t, i=i: t.transpose(
                                  out=ptr[:, 0:128],
                                  in_=biasb[:, 128 * i:128 * (i + 1)],
                                  identity=ident[:]))
                    sc = 0.125 if scaleq else 1.0
                    mE = step("vector", [(T, mt)],
                              lambda v, oc=oc, sc=sc: v.tensor_scalar(
                                  out=biasf[:, 4 * oc:4 * oc + 4],
                                  in0=ptr[:, 0:4], scalar1=sc, scalar2=None,
                                  op0=ALU.mult))
            return mE

        wq_ftr = {}

        def wslab_quant(wa_dram_j):
            ml = dma1("sync", [(V, V.n), (A, A.n)], wa_sb[:], wa_dram_j)

            def fstat(v):
                v.reduce_max(out=wsc[0][:], in_=wa_sb[:], axis=AX.X,
                             apply_absolute_value=True)
                v.drain()
                v.tensor_scalar(out=wsc[1][:], in0=wsc[0][:], scalar1=7e-9,
                                scalar2=float(1.0 / 7.0), op0=ALU.max, op1=ALU.mult)
                v.drain()
                v.reciprocal(out=wsc[2][:], in_=wsc[1][:])
                v.drain()
                v.tensor_scalar(out=wsc[3][:], in0=wsc[1][:], scalar1=-8.0,
                                scalar2=None, op0=ALU.mult)
            mv = step("vector", [(LI, ml), (G, G.n)], fstat)
            mvE = V.n
            for hh in range(4):
                ma = step("scalar", [(V, mv), (V, V.n)],
                          lambda a, hh=hh: a.activation(
                              out=qu[:], in_=wa_sb[:, 512 * hh:512 * (hh + 1)],
                              func=AF.Identity, scale=wsc[2][:]))

                def fcvt(v):
                    v.tensor_copy(out=qi[:], in_=qu[:])
                    v.drain()
                    v.tensor_copy(out=qf[:], in_=qi[:])
                mg = step("vector", [(A, ma)], fcvt)
                ma2 = step("scalar", [(V, mg), (T, wq_ftr.get(hh, 0))],
                           lambda a, hh=hh: a.activation(
                               out=wdq[:, 512 * hh:512 * (hh + 1)], in_=qf[:],
                               func=AF.Identity, scale=wsc[1][:]))

                def ftr(t, hh=hh):
                    for j in range(4):
                        k = 4 * hh + j
                        t.transpose(out=ptr[:, 128 * j:128 * (j + 1)],
                                    in_=wdq[:, 128 * k:128 * (k + 1)],
                                    identity=ident[:])
                mt = step("tensor", [(A, ma2), (V, mvE)], ftr)
                wq_ftr[hh] = mt

                def fev(v, hh=hh):
                    for j in range(4):
                        k = 4 * hh + j
                        v.tensor_copy(out=wdqB[:, 128 * k:128 * (k + 1)],
                                      in_=ptr[:, 128 * j:128 * (j + 1)])
                mvE = step("vector", [(T, mt)], fev)
            return mvE

        STOP = int(os.environ.get("KSTOP", "9"))

        def dump_sb(name, src_ap, deps):
            if KDUMP and name in dmp:
                dma1("sync", deps, dmp[name], src_ap)

        def dump_dr(name, src_ap, deps):
            if KDUMP and name in dmp:
                dma1("sync", deps, dmp[name], src_ap)

        def finish_early():
            mo = 0
            for i in range(8):
                mlx = dma1("sync", [(V, V.n), (A, A.n)], xt[:],
                           xo[128 * i:128 * (i + 1), :])
                mo = dma1("sync", [(LI, mlx)], yo[128 * i:128 * (i + 1), :], xt[:])

            def ffin(s_):
                s_.wait_ge(LI.h, mo)
            blk.sync(ffin)

        KSKIP = os.environ.get("KSKIP", "")
        if "ln" in KSKIP:
            m_xn = V.n
        else:
            m_xn = ln_to_xn(xo, [(LI, m_ident), (V, m_init2)])
        if "quant" in KSKIP:
            m_q = V.n
        else:
            m_q = quantize(xn, res)
        dump_sb("d_xn", xn[:], [(V, m_xn)])
        dump_sb("d_res", res, [(V, m_q)])
        dump_sb("d_cb", cbf32[:], [(V, m_q)])
        dump_sb("d_cmax", cmaxQ[:], [(V, m_q)])
        dump_sb("d_s", sQ[:], [(V, m_q)])
        dump_sb("d_idx", idxQ[:], [(V, m_q)])
        dump_sb("d_tmax", tmaxQ[:], [(V, m_q)])
        if STOP <= 1:
            finish_early()
            return nc

        def l0_proj(pi, kind):
            mB = cb_bias_rows(wB[pi], l0b[pi], scaleq=(kind == "q"))
            mo = 0
            mW = wslab_quant(wA[pi][0])
            for j in range(16):
                def fmm(t, j=j):
                    for grp in range(2):
                        for k in range(16):
                            t.matmul(out=pb[grp][:],
                                     lhsT=wdqB[:, 128 * k:128 * (k + 1)],
                                     rhs=res[:, NT * k + 512 * grp:NT * k + 512 * (grp + 1)],
                                     start=(k == 0), stop=(k == 15))
                mt = step("tensor", [(V, mW), (V, m_q), (V, mB), (A, A.n)], fmm)
                if j < 15:
                    mW = wslab_quant(wA[pi][j + 1])

                def fev(a, j=j):
                    sc = 0.125 if kind == "q" else 1.0
                    for grp in range(2):
                        for c2 in range(2):
                            c = 2 * grp + c2
                            a.activation(
                                out=evst[:, 512 * grp + 256 * c2:512 * grp + 256 * (c2 + 1)],
                                in_=pb[grp][:, 256 * c2:256 * (c2 + 1)],
                                func=AF.Identity, scale=sc,
                                bias=biasf[:, 4 * j + c:4 * j + c + 1])
                ma = step("scalar", [(T, mt), (LI, LI.n)], fev)
                if kind == "q":
                    mo = dma("sync", [(A, ma)],
                             [(qdr[2 * j, :, :], evst[0:64, :]),
                              (qdr[2 * j + 1, :, :], evst[64:128, :])])
                else:
                    r0 = 0 if kind == "k" else 64
                    mo = dma("sync", [(A, ma)],
                             [(kvs[2 * j, r0:r0 + 64, :], evst[0:64, :]),
                              (kvs[2 * j + 1, r0:r0 + 64, :], evst[64:128, :])])
            return mo

        if "proj" not in KSKIP:
            l0_proj(0, "q")
            l0_proj(1, "k")
            l0_proj(2, "v")
        m_kv = LI.n
        dump_dr("d_kvs", kvs, [(LI, m_kv)])
        dump_dr("d_qdr", qdr, [(LI, m_kv)])
        m_kv = LI.n
        if STOP <= 2:
            finish_early()
            return nc

        KCC = os.environ.get("KCC", "full")
        if KCC == "tiny":
            ccs = nc.dram_tensor("ccs", [128, 256], F32).ap()
            ccg = nc.dram_tensor("ccg", [256, 256], F32).ap()
            m_cc0 = step("gpsimd", [(LI, m_kv)],
                         lambda g: g.collective_compute(
                             "AllGather", ALU.bypass,
                             replica_groups=[[0, 1], [2, 3], [4, 5], [6, 7]],
                             ins=[ccs], outs=[ccg]))
        elif KCC == "none":
            m_cc0 = step("gpsimd", [(LI, m_kv)], lambda g: g.engine_nop())
        else:
            m_cc0 = step("gpsimd", [(LI, m_kv)],
                         lambda g: g.collective_compute(
                             "AllGather", ALU.bypass,
                             replica_groups=[[0, 1], [2, 3], [4, 5], [6, 7]],
                             ins=[kvs], outs=[kvg]), raw=True)
        if STOP == 25:
            mwait = step("gpsimd", [(G, m_cc0)], lambda g: g.engine_nop())
            finish_early()
            return nc
        if STOP == 26:
            finish_early()
            return nc

        def l0_ffn():
            mdff = V.n
            for ih in range(2):
                for jsl in range(8):
                    js = 8 * ih + jsl
                    mw = dma("gpsimd", [(T, T.n)],
                             [(wgs[:], wg_in[js]), (wus[:], wu_in[js])])
                    for tt in range(8):
                        def fmm(t, tt=tt):
                            for k in range(16):
                                lhs = xn[:, NT * k + 128 * tt:NT * k + 128 * (tt + 1)]
                                t.matmul(out=pb[0][:], lhsT=lhs,
                                         rhs=wgs[:, 512 * k:512 * (k + 1)],
                                         start=(k == 0), stop=(k == 15))
                                t.matmul(out=pb[1][:], lhsT=lhs,
                                         rhs=wus[:, 512 * k:512 * (k + 1)],
                                         start=(k == 0), stop=(k == 15))
                        mt = step("tensor", [(LW, mw), (V, m_xn), (A, A.n),
                                             (V, V.n)], fmm)
                        ma = step("scalar", [(T, mt), (V, V.n), (LI, LI.n)],
                                  lambda a: a.activation(out=evst[:, 0:512],
                                                         in_=pb[0][:],
                                                         func=AF.Silu))
                        mvh = step("vector", [(A, ma)],
                                   lambda v, tt=tt: v.tensor_tensor(
                                       out=hh2[tt % 2][:],
                                       in0=pb[1][:], in1=evst[:, 0:512],
                                       op=ALU.mult))
                        ma2 = step("scalar", [(V, mvh)],
                                   lambda a, tt=tt: a.activation(
                                       out=s2[:], in_=hh2[tt % 2][:],
                                       func=AF.Square))

                        def fsp(v, tt=tt):
                            q = [s2[:, 128 * i:128 * (i + 1)] for i in range(4)]
                            v.tensor_tensor(out=t4[0][:], in0=q[0], in1=q[1], op=ALU.max)
                            v.tensor_tensor(out=t4[1][:], in0=q[0], in1=q[1], op=ALU.min)
                            v.tensor_tensor(out=t4[2][:], in0=q[2], in1=q[3], op=ALU.max)
                            v.tensor_tensor(out=t4[3][:], in0=q[2], in1=q[3], op=ALU.min)
                            v.drain()
                            v.tensor_tensor(out=t4[0][:], in0=t4[0][:], in1=t4[2][:],
                                            op=ALU.min)
                            v.tensor_tensor(out=t4[1][:], in0=t4[1][:], in1=t4[3][:],
                                            op=ALU.max)
                            v.drain()
                            v.tensor_tensor(out=t4[0][:], in0=t4[0][:], in1=t4[1][:],
                                            op=ALU.max)
                            v.drain()
                            for i in range(4):
                                v.tensor_tensor(out=t4[2][:],
                                                in0=s2[:, 128 * i:128 * (i + 1)],
                                                in1=t4[0][:], op=ALU.is_ge)
                                v.drain()
                                v.tensor_tensor(out=hh2[tt % 2][:, 128 * i:128 * (i + 1)],
                                                in0=hh2[tt % 2][:, 128 * i:128 * (i + 1)],
                                                in1=t4[2][:], op=ALU.mult)
                                v.drain()
                        mv = step("vector", [(A, ma2)], fsp)

                        def ftr(t, tt=tt):
                            for i in range(4):
                                t.transpose(out=ptr[:, 128 * i:128 * (i + 1)],
                                            in_=hh2[tt % 2][:, 128 * i:128 * (i + 1)],
                                            identity=ident[:])
                        mt2 = step("tensor", [(V, mv), (V, V.n)], ftr)

                        def fev(v, jsl=jsl, tt=tt):
                            for i in range(4):
                                ktl = 4 * jsl + i
                                v.tensor_copy(
                                    out=hspI[:, 1024 * ktl + 128 * tt:1024 * ktl + 128 * (tt + 1)],
                                    in_=ptr[:, 128 * i:128 * (i + 1)])
                        step("vector", [(T, mt2)], fev)
                m_hsp = V.n
                for oc in range(16):
                    mw = dma1("gpsimd", [(T, T.n)], wds[:, 0:4096],
                              wd_in[oc][:, 4096 * ih:4096 * (ih + 1)])

                    def fmm(t, oc=oc):
                        for ktl in range(32):
                            for grp in range(2):
                                t.matmul(out=pb[2 + grp][:],
                                         lhsT=wds[:, 128 * ktl:128 * (ktl + 1)],
                                         rhs=hspI[:, 1024 * ktl + 512 * grp:1024 * ktl + 512 * (grp + 1)],
                                         start=(ktl == 0), stop=(ktl == 31))
                    mt = step("tensor", [(LW, mw), (V, m_hsp), (V, V.n)], fmm)

                    def fev(v, oc=oc, ih=ih):
                        for grp in range(2):
                            dsl = dff[:, NT * oc + 512 * grp:NT * oc + 512 * (grp + 1)]
                            if ih == 0:
                                v.tensor_copy(out=dsl, in_=pb[2 + grp][:])
                            else:
                                v.tensor_tensor(out=dsl, in0=dsl, in1=pb[2 + grp][:],
                                                op=ALU.add)
                    mdff = step("vector", [(T, mt)], fev)
            return mdff

        m_dff = l0_ffn()
        dump_sb("d_dff", dff[:], [(V, m_dff)])
        if STOP <= 3:
            finish_early()
            return nc

        def l0_mix():
            def fz(v):
                v.memset(khb[64:128, :], 0.0)
                v.memset(vhb[64:128, :], 0.0)
                v.memset(qhb[64:128, :], 0.0)
            step("vector", [(T, T.n), (A, A.n)], fz)
            mctx = V.n
            for h in range(H0):
                mlk = dma("sync", [(V, V.n), (T, T.n), (G, m_cc0), (A, A.n),
                                   (LI, m_kv)],
                          [(khb[0:64, 1024 * r:1024 * (r + 1)], kvg[32 * r + h, 0:64, :])
                           for r in range(2)]
                          + [(vhb[0:64, 1024 * r:1024 * (r + 1)], kvg[32 * r + h, 64:128, :])
                             for r in range(2)]
                          + [(qhb[0:64, :], qdr[h])])
                mvV = V.n
                for b in range(2):
                    def ftr(t, b=b):
                        for i in range(8):
                            kt = 8 * b + i
                            t.transpose(out=ptr[:, 128 * i:128 * (i + 1)],
                                        in_=vhb[:, 128 * kt:128 * (kt + 1)],
                                        identity=ident[:])
                    mt = step("tensor", [(LI, mlk), (V, mvV)], ftr)

                    def fevV(v, b=b):
                        for i in range(8):
                            kt = 8 * b + i
                            v.tensor_copy(out=vaug[:, 128 * kt:128 * kt + 64],
                                          in_=ptr[:, 128 * i:128 * i + 64])
                    mvV = step("vector", [(T, mt)], fevV)

                def fones(v):
                    for kt in range(16):
                        v.memset(vaug[:, 128 * kt + 64:128 * kt + 65], 1.0)
                mvV = step("vector", [(V, mvV)], fones)
                maS = step("scalar", [(LI, mlk), (T, T.n), (V, mvV)],
                           lambda a: a.activation(out=sqb[:], in_=khb[:],
                                                  func=AF.Square))
                mvn = V.n
                for g in range(4):
                    mt = step("tensor", [(A, maS), (V, mvn)],
                              lambda t, g=g: t.matmul(out=pb[4][:], lhsT=ones_bf[:],
                                                      rhs=sqb[:, 512 * g:512 * (g + 1)],
                                                      start=True, stop=True))

                    def fmx(v, g=g):
                        v.reduce_max(out=c1[0:1, :], in_=pb[4][0:1, :], axis=AX.X)
                        v.drain()
                        if g == 0:
                            v.tensor_copy(out=c0[0:1, :], in_=c1[0:1, :])
                        else:
                            v.tensor_tensor(out=c0[0:1, :], in0=c0[0:1, :],
                                            in1=c1[0:1, :], op=ALU.max)
                    mvn = step("vector", [(T, mt)], fmx)
                maS = step("scalar", [(LI, mlk), (T, T.n)],
                           lambda a: a.activation(out=sqb[:, 0:1024], in_=qhb[:],
                                                  func=AF.Square))
                for g in range(2):
                    mt = step("tensor", [(A, maS), (V, mvn)],
                              lambda t, g=g: t.matmul(out=pb[4][:], lhsT=ones_bf[:],
                                                      rhs=sqb[:, 512 * g:512 * (g + 1)],
                                                      start=True, stop=True))

                    def fmx2(v, g=g):
                        v.reduce_max(out=c1[0:1, :], in_=pb[4][0:1, :], axis=AX.X)
                        v.drain()
                        if g == 0:
                            v.tensor_copy(out=tmpc[0:1, :], in_=c1[0:1, :])
                        else:
                            v.tensor_tensor(out=tmpc[0:1, :], in0=tmpc[0:1, :],
                                            in1=c1[0:1, :], op=ALU.max)
                    mvn = step("vector", [(T, mt)], fmx2)

                def fB(v):
                    v.tensor_tensor(out=c0[0:1, :], in0=c0[0:1, :], in1=tmpc[0:1, :],
                                    op=ALU.mult)
                    v.drain()
                    v.tensor_copy(out=st0r[0:1, 0:1], in_=c0[0:1, :])
                mvn = step("vector", [], fB)
                mt = step("tensor", [(V, mvn)],
                          lambda t: t.matmul(out=pb[4][:, 0:128], lhsT=ones_bf[:],
                                             rhs=st0r[:], start=True, stop=True))
                mvn = step("vector", [(T, mt)],
                           lambda v: v.tensor_copy(out=c0[:], in_=pb[4][:, 0:1]))
                ma3 = step("scalar", [(V, mvn)],
                           lambda a: a.activation(out=nbias[:], in_=c0[:],
                                                  func=AF.Sqrt))
                mvn = step("vector", [(A, ma3)],
                           lambda v: v.tensor_scalar(out=nbias[:], in0=nbias[:],
                                                     scalar1=-1.0, scalar2=None,
                                                     op0=ALU.mult))
                for qt in range(2):
                    mexp = A.n
                    for kt in range(16):
                        bank = kt % 2
                        mt = step("tensor", [(LI, mlk), (V, mvn), (A, mexp)],
                                  lambda t, kt=kt, qt=qt, bank=bank: t.matmul(
                                      out=pb[bank][:],
                                      lhsT=khb[:, 128 * kt:128 * (kt + 1)],
                                      rhs=qhb[:, 512 * qt:512 * (qt + 1)],
                                      start=True, stop=True))
                        mexp = step("scalar", [(T, mt)],
                                    lambda a, kt=kt, bank=bank: a.activation(
                                        out=exps[:, 512 * kt:512 * (kt + 1)],
                                        in_=pb[bank][:], func=AF.Exp,
                                        bias=nbias[:], scale=1.0))

                    def fpv(t):
                        for kt in range(16):
                            t.matmul(out=pb[2][0:65, :],
                                     lhsT=vaug[:, 128 * kt:128 * kt + 65],
                                     rhs=exps[:, 512 * kt:512 * (kt + 1)],
                                     start=(kt == 0), stop=(kt == 15))
                    mt = step("tensor", [(A, mexp), (V, mvV), (V, V.n)], fpv)

                    def frc(v):
                        v.reciprocal(out=srow[0:1, :], in_=pb[2][64:65, :])
                        v.drain()
                        v.tensor_copy(out=st0[0:1, :], in_=srow[0:1, :])
                    mv1 = step("vector", [(T, mt)], frc)
                    mt2 = step("tensor", [(V, mv1)],
                               lambda t: t.matmul(out=pb[3][:], lhsT=ones_bf[:],
                                                  rhs=st0[:], start=True, stop=True))
                    mv2 = step("vector", [(T, mt2)],
                               lambda v: v.tensor_copy(out=rkst[:], in_=pb[3][:]))

                    def fctx(v, h=h, qt=qt):
                        fc = h // 2
                        ro = 64 * (h % 2)
                        v.tensor_tensor(
                            out=ctxb[ro:ro + 64, NT * fc + 512 * qt:NT * fc + 512 * (qt + 1)],
                            in0=pb[2][0:64, :], in1=rkst[0:64, :], op=ALU.mult)
                    mctx = step("vector", [(V, mv2)], fctx)
            return mctx

        m_ctx = l0_mix()
        dump_sb("d_ctx", ctxb, [(V, m_ctx)])

        m_q2 = quantize(ctxb, res)
        mB_o = cb_bias_rows(wB[3], l0b[3], scaleq=False)

        def l0_out():
            mE = V.n
            mW = wslab_quant(wA[3][0])
            for j in range(16):
                def fmm(t, j=j):
                    for grp in range(2):
                        for k in range(16):
                            t.matmul(out=pb[grp][:],
                                     lhsT=wdqB[:, 128 * k:128 * (k + 1)],
                                     rhs=res[:, NT * k + 512 * grp:NT * k + 512 * (grp + 1)],
                                     start=(k == 0), stop=(k == 15))
                mt = step("tensor", [(V, mW), (V, m_q2), (V, mB_o), (A, A.n),
                                     (V, V.n)], fmm)
                if j < 15:
                    mW = wslab_quant(wA[3][j + 1])

                def fev(v, j=j):
                    for grp in range(2):
                        for c2 in range(2):
                            c = 2 * grp + c2
                            sl = dff[:, NT * j + 512 * grp + 256 * c2:NT * j + 512 * grp + 256 * (c2 + 1)]
                            v.scalar_tensor_tensor(
                                out=sl, in0=pb[grp][:, 256 * c2:256 * (c2 + 1)],
                                scalar=biasf[:, 4 * j + c:4 * j + c + 1],
                                in1=sl, op0=ALU.add, op1=ALU.add)
                mE = step("vector", [(T, mt), (V, m_dff), (LI, LI.n)], fev)
            mo = 0
            for i in range(8):
                mlx = dma1("sync", [(V, V.n), (A, A.n)], xt[:],
                           xo[128 * i:128 * (i + 1), :])
                mE2 = V.n
                for b in range(2):
                    def ftr(t, i=i, b=b):
                        for j in range(8):
                            k = 8 * b + j
                            t.transpose(out=ptr[:, 128 * j:128 * (j + 1)],
                                        in_=dff[:, NT * k + 128 * i:NT * k + 128 * (i + 1)],
                                        identity=ident[:])
                    mt = step("tensor", [(V, mE), (V, V.n)], ftr)

                    def fevd(v, b=b):
                        for j in range(8):
                            k = 8 * b + j
                            v.tensor_copy(out=dtm[:, 128 * k:128 * (k + 1)],
                                          in_=ptr[:, 128 * j:128 * (j + 1)])
                    mE2 = step("vector", [(T, mt)], fevd)
                mv = step("vector", [(V, mE2), (LI, mlx)],
                          lambda v: v.tensor_tensor(out=xt[:], in0=xt[:],
                                                    in1=dtm, op=ALU.add))
                mo = dma1("sync", [(V, mv)], x1own[128 * i:128 * (i + 1), :], xt[:])
            return mo

        m_x1 = l0_out()
        dump_dr("d_x1", x1own, [(LI, m_x1)])
        m_x1 = LI.n if KDUMP else m_x1
        if STOP <= 4:
            finish_early()
            return nc

        m_xn1 = ln_to_xn(x1own, [(LI, m_x1)])

        def l1_proj(pi, kind):
            mo = 0
            for j in range(16):
                mw = dma1("gpsimd", [(T, T.n)], wgs[:, 0:2048], w1_in[pi][j])

                def fmm(t, j=j):
                    for grp in range(2):
                        for k in range(16):
                            t.matmul(out=pb[grp][:],
                                     lhsT=wgs[:, 128 * k:128 * (k + 1)],
                                     rhs=xn[:, NT * k + 512 * grp:NT * k + 512 * (grp + 1)],
                                     start=(k == 0), stop=(k == 15))
                mt = step("tensor", [(LW, mw), (V, m_xn1), (A, A.n), (V, V.n)], fmm)
                if kind == "v":
                    def fevv(a):
                        a.activation(out=evst[:, 0:512], in_=pb[0][:],
                                     func=AF.Identity)
                        a.drain()
                        a.activation(out=evst[:, 512:1024], in_=pb[1][:],
                                     func=AF.Identity)
                    ma = step("scalar", [(T, mt), (LI, LI.n)], fevv)
                    mo = dma1("sync", [(A, ma)], kvs1[j, 128:256, :], evst[:])
                elif kind == "k":
                    def fevk(a):
                        a.activation(out=evst[:, 0:512], in_=pb[0][:],
                                     func=AF.Identity, scale=gcol[:], bias=bcol[:])
                        a.drain()
                        a.activation(out=evst[:, 512:1024], in_=pb[1][:],
                                     func=AF.Identity, scale=gcol[:], bias=bcol[:])
                    ma = step("scalar", [(T, mt), (LI, max(m_gb, LI.n))], fevk)
                    mo = dma1("sync", [(A, ma)], kvs1[j, 0:128, :], evst[:])
                else:
                    def fevq(a):
                        a.activation(out=evst[:, 0:512], in_=pb[0][:],
                                     func=AF.Identity, scale=gcol[:], bias=bcol[:])
                        a.drain()
                        a.activation(out=evst[:, 512:1024], in_=pb[1][:],
                                     func=AF.Identity, scale=gcol[:], bias=bcol[:])
                    ma = step("scalar", [(T, mt), (LI, max(m_gb, LI.n))], fevq)
                    mvq = V.n
                    for grp in range(2):
                        mt2 = step("tensor", [(A, ma), (V, V.n)],
                                   lambda t, grp=grp: t.matmul(
                                       out=pb[2][:], lhsT=ones_bf[:],
                                       rhs=evst[:, 512 * grp:512 * (grp + 1)],
                                       start=True, stop=True))
                        mvq = step("vector", [(T, mt2)],
                                   lambda v, grp=grp: v.scalar_tensor_tensor(
                                       out=evst[:, 512 * grp:512 * (grp + 1)],
                                       in0=pb[2][:], scalar=float(-1.0 / 128.0),
                                       in1=evst[:, 512 * grp:512 * (grp + 1)],
                                       op0=ALU.mult, op1=ALU.add))
                    mo = dma1("sync", [(V, mvq), (A, A.n)], qdr1[j], evst[:])
            return mo

        l1_proj(0, "q")
        l1_proj(1, "k")
        l1_proj(2, "v")
        m_kv1 = LI.n
        dump_dr("d_kvs1", kvs1, [(LI, m_kv1)])
        dump_dr("d_qdr1", qdr1, [(LI, m_kv1)])
        m_kv1 = LI.n

        m_cc1 = step("gpsimd", [(LI, m_kv1)],
                     lambda g: g.collective_compute(
                         "AllGather", ALU.bypass,
                         replica_groups=[[0, 1], [2, 3], [4, 5], [6, 7]],
                         ins=[kvs1], outs=[kvg1]), raw=True)

        def l1_base():
            mE = V.n
            for j in range(16):
                mw = dma1("gpsimd", [(T, T.n)], wgs[:, 0:2048], wf1[j])

                def fmm(t, j=j):
                    for grp in range(2):
                        for k in range(16):
                            t.matmul(out=pb[grp][:],
                                     lhsT=wgs[:, 128 * k:128 * (k + 1)],
                                     rhs=xn[:, NT * k + 512 * grp:NT * k + 512 * (grp + 1)],
                                     start=(k == 0), stop=(k == 15))
                mt = step("tensor", [(LW, mw), (V, m_xn1), (A, A.n), (V, V.n)], fmm)

                def fev(v, j=j):
                    for grp in range(2):
                        v.tensor_scalar(
                            out=dff[:, NT * j + 512 * grp:NT * j + 512 * (grp + 1)],
                            in0=pb[grp][:], scalar1=b1c[:, j:j + 1], scalar2=None,
                            op0=ALU.add)
                mE = step("vector", [(T, mt), (V, m_b1c)], fev)
            return mE

        m_base = l1_base()
        dump_sb("d_dffB", dff[:], [(V, m_base)])

        def l1_gate_lora():
            mgw = dma1("gpsimd", [(T, T.n)], wds[:, 0:2048], gwT_in)
            mla = dma1("gpsimd", [(T, T.n)], wds[:, 2048:4096], laT_in)
            mlb2 = dma1("gpsimd", [(T, T.n)], wds[:, 4096:6144], lbS_in)
            mgbs = step("vector", [(LI, LI.n), (T, T.n)],
                        lambda v: v.memset(st0r[:], 0.0))
            mgbl = dma1("gpsimd", [(V, mgbs)], st0r[0:1, 0:8],
                        gbv.rearrange("(o n) -> o n", o=1))
            for tt in range(8):
                def fmm(t, tt=tt):
                    for k in range(16):
                        t.matmul(out=pb[4][:, 0:128],
                                 lhsT=xn[:, NT * k + 128 * tt:NT * k + 128 * (tt + 1)],
                                 rhs=wds[:, 128 * k:128 * (k + 1)],
                                 start=(k == 0), stop=False)
                    t.matmul(out=pb[4][:, 0:128], lhsT=ones_bf[:],
                             rhs=st0r[:], start=False, stop=True)
                mt = step("tensor", [(LW, max(mgw, mgbl)), (V, m_xn1), (V, V.n),
                                     (A, A.n)], fmm)

                def fgate(v):
                    lg = pb[4][:, 0:8]
                    v.reduce_max(out=c0[:], in_=lg, axis=AX.X)
                    v.drain()
                    v.tensor_scalar(out=t4[0][:, 0:8], in0=lg, scalar1=c0[:],
                                    scalar2=None, op0=ALU.is_ge)
                    v.drain()
                    v.scalar_tensor_tensor(out=t4[1][:, 0:8], in0=t4[0][:, 0:8],
                                           scalar=-1e30, in1=lg, op0=ALU.mult,
                                           op1=ALU.add)
                    v.drain()
                    v.reduce_max(out=c1[:], in_=t4[1][:, 0:8], axis=AX.X)
                    v.drain()
                    v.tensor_scalar(out=t4[2][:, 0:8], in0=lg, scalar1=c1[:],
                                    scalar2=None, op0=ALU.is_ge)
                    v.drain()
                    v.tensor_tensor(out=tmpc[:], in0=c1[:], in1=c0[:],
                                    op=ALU.subtract)
                    v.drain()
                    v.tensor_scalar(out=nmu[:], in0=c0[:], scalar1=-1.0,
                                    scalar2=None, op0=ALU.mult)
                mv = step("vector", [(T, mt)], fgate)

                def fexp(a):
                    a.activation(out=t4[3][:, 0:8], in_=pb[4][:, 0:8], func=AF.Exp,
                                 bias=nmu[:], scale=1.0)
                    a.drain()
                    a.activation(out=rstd[:], in_=tmpc[:], func=AF.Exp)
                ma = step("scalar", [(V, mv)], fexp)

                def fw(v, tt=tt):
                    v.tensor_scalar(out=rstd[:], in0=rstd[:], scalar1=1.0,
                                    scalar2=None, op0=ALU.add)
                    v.drain()
                    v.reciprocal(out=tmpc[:], in_=rstd[:])
                    v.drain()
                    v.tensor_scalar(out=t4[3][:, 0:8], in0=t4[3][:, 0:8],
                                    scalar1=tmpc[:], scalar2=None, op0=ALU.mult)
                    v.drain()
                    v.tensor_tensor(out=wtok[tt][:, 0:8], in0=t4[3][:, 0:8],
                                    in1=t4[2][:, 0:8], op=ALU.mult)
                step("vector", [(A, ma)], fw)
            mvz = step("vector", [(LI, LI.n)], lambda v: v.memset(evst[:], 0.0))
            for b in range(2):
                def ftr(t, b=b):
                    for i in range(4):
                        tt = 4 * b + i
                        t.transpose(out=ptr[:, 128 * i:128 * (i + 1)],
                                    in_=wtok[tt][:], identity=ident[:])
                mt = step("tensor", [(V, V.n)], ftr)

                def fev(v, b=b):
                    for i in range(4):
                        tt = 4 * b + i
                        v.tensor_copy(out=evst[0:8, 128 * tt:128 * (tt + 1)],
                                      in_=ptr[0:8, 128 * i:128 * (i + 1)])
                mvz = step("vector", [(T, mt)], fev)
            for tt in range(8):
                def fxa(t, tt=tt):
                    for k in range(16):
                        t.matmul(out=pb[4][:, 0:128],
                                 lhsT=wds[:, 2048 + 128 * k:2048 + 128 * (k + 1)],
                                 rhs=xn[:, NT * k + 128 * tt:NT * k + 128 * (tt + 1)],
                                 start=(k == 0), stop=(k == 15))
                mt = step("tensor", [(LW, mla), (V, m_xn1), (V, V.n)], fxa)
                mt2 = step("tensor", [(V, mvz), (LI, m_ident)],
                           lambda t, tt=tt: t.matmul(
                               out=pb[5][:, 0:128], lhsT=sel8[:],
                               rhs=evst[:, 128 * tt:128 * (tt + 1)],
                               start=True, stop=True))
                mv = step("vector", [(T, mt2)],
                          lambda v: v.tensor_copy(out=rkst[:, 0:128],
                                                  in_=pb[5][:, 0:128]))
                step("vector", [(V, mv), (T, mt)],
                     lambda v, tt=tt: v.tensor_tensor(
                         out=sqb[:, 128 * tt:128 * (tt + 1)],
                         in0=pb[4][:, 0:128], in1=rkst[:, 0:128], op=ALU.mult))
            m_xa = V.n
            mE = V.n
            for j in range(16):
                def flu(t, j=j):
                    for grp in range(2):
                        t.matmul(out=pb[2 + grp][:],
                                 lhsT=wds[:, 4096 + 128 * j:4096 + 128 * (j + 1)],
                                 rhs=sqb[:, 512 * grp:512 * (grp + 1)],
                                 start=True, stop=True)
                mt = step("tensor", [(LW, mlb2), (V, m_xa), (V, V.n)], flu)

                def fad(v, j=j):
                    import os as _o
                    mode = _o.environ.get("KFAD", "add")
                    for grp in range(2):
                        sl = dff[:, NT * j + 512 * grp:NT * j + 512 * (grp + 1)]
                        if mode == "copy":
                            v.tensor_copy(out=sl, in_=pb[2 + grp][:])
                        elif mode == "skip":
                            pass
                        else:
                            v.tensor_tensor(out=sl, in0=sl, in1=pb[2 + grp][:],
                                            op=ALU.add)
                mE = step("vector", [(T, mt), (V, m_base), (LI, LI.n)], fad)
            return mE

        m_dff1 = l1_gate_lora()
        dump_sb("d_dff1", dff[:], [(V, m_dff1)])
        if STOP <= 5:
            finish_early()
            return nc

        def l1_mix():
            mctx = V.n
            for h in range(H1):
                pairs = []
                for r in range(2):
                    for i4 in range(4):
                        g = 2 * i4 + r
                        pairs.append((khb[:, 256 * g:256 * (g + 1)],
                                      kvg1[16 * r + h, 0:128, 256 * i4:256 * (i4 + 1)]))
                        pairs.append((vhb[:, 256 * g:256 * (g + 1)],
                                      kvg1[16 * r + h, 128:256, 256 * i4:256 * (i4 + 1)]))
                mlk = dma("sync", [(V, V.n), (T, T.n), (G, m_cc1), (A, A.n),
                                   (LI, m_kv1)],
                          pairs + [(qhb[:], qdr1[h])])
                mvV = V.n
                for b in range(2):
                    def ftr(t, b=b):
                        for i in range(8):
                            kt = 8 * b + i
                            t.transpose(out=ptr[:, 128 * i:128 * (i + 1)],
                                        in_=vhb[:, 128 * kt:128 * (kt + 1)],
                                        identity=ident[:])
                    mt = step("tensor", [(LI, mlk), (V, mvV)], ftr)

                    def fevV(v, b=b):
                        for i in range(8):
                            kt = 8 * b + i
                            v.tensor_copy(out=vaug[:, 128 * kt:128 * (kt + 1)],
                                          in_=ptr[:, 128 * i:128 * (i + 1)])
                    mvV = step("vector", [(T, mt)], fevV)
                maQ = step("scalar", [(LI, mlk), (T, T.n), (V, mvV)],
                           lambda a: a.activation(out=sqb[:, 0:1024], in_=qhb[:],
                                                  func=AF.Square))
                mvz = V.n
                for grp in range(2):
                    mtq = step("tensor", [(A, maQ), (V, mvz)],
                               lambda t, grp=grp: t.matmul(
                                   out=pb[4][:], lhsT=ones_bf[:],
                                   rhs=sqb[:, 512 * grp:512 * (grp + 1)],
                                   start=True, stop=True))
                    mvz = step("vector", [(T, mtq)],
                               lambda v, grp=grp: v.tensor_scalar(
                                   out=zrow[0:1, 512 * grp:512 * (grp + 1)],
                                   in0=pb[4][0:1, :],
                                   scalar1=float(1.0 / 128.0), scalar2=EPS,
                                   op0=ALU.mult, op1=ALU.add))
                maK = step("scalar", [(LI, mlk), (T, T.n), (V, mvV)],
                           lambda a: a.activation(out=sqb[:], in_=khb[:],
                                                  func=AF.Square))
                mrk = V.n
                for g in range(4):
                    mtS = step("tensor", [(LI, mlk), (V, mrk)],
                               lambda t, g=g: t.matmul(out=pb[4][:], lhsT=ones_bf[:],
                                                       rhs=khb[:, 512 * g:512 * (g + 1)],
                                                       start=True, stop=True))
                    mtS2 = step("tensor", [(A, maK)],
                                lambda t, g=g: t.matmul(out=pb[5][:], lhsT=ones_bf[:],
                                                        rhs=sqb[:, 512 * g:512 * (g + 1)],
                                                        start=True, stop=True))

                    def fvar(v):
                        v.tensor_scalar(out=muq, in0=pb[4][:],
                                        scalar1=float(1.0 / 128.0), scalar2=None,
                                        op0=ALU.mult)
                        v.drain()
                        v.tensor_tensor(out=muq, in0=muq, in1=muq,
                                        op=ALU.mult)
                        v.drain()
                        v.scalar_tensor_tensor(out=muq, in0=pb[5][:],
                                               scalar=float(1.0 / 128.0), in1=muq,
                                               op0=ALU.mult, op1=ALU.subtract)
                        v.drain()
                        v.tensor_scalar(out=muq, in0=muq, scalar1=EPS,
                                        scalar2=None, op0=ALU.add)
                        v.drain()
                        v.reciprocal(out=muq, in_=muq)
                    mv = step("vector", [(T, mtS), (T, mtS2)], fvar)
                    ma2 = step("scalar", [(V, mv)],
                               lambda a: a.activation(out=muq, in_=muq,
                                                      func=AF.Sqrt))
                    mv2 = step("vector", [(A, ma2)],
                               lambda v: v.tensor_scalar(
                                   out=rkst[0:1, :], in0=muq[0:1, 0:512],
                                   scalar1=float(1.0 / np.sqrt(128.0)),
                                   scalar2=None, op0=ALU.mult))

                    def ftr2(t):
                        for i in range(4):
                            t.transpose(out=ptr[:, 128 * i:128 * (i + 1)],
                                        in_=rkst[:, 128 * i:128 * (i + 1)],
                                        identity=ident[:])
                    mt3 = step("tensor", [(V, mv2)], ftr2)

                    def fev3(v, g=g):
                        for i in range(4):
                            v.tensor_copy(out=rk[:, 4 * g + i:4 * g + i + 1],
                                          in_=ptr[:, 128 * i:128 * i + 1])
                    mrk = step("vector", [(T, mt3)], fev3)
                for qi_ in range(4):
                    ntr = 4 * qi_ + 4
                    maE = A.n
                    for kt in range(ntr):
                        bank = kt % 2
                        mt = step("tensor", [(LI, mlk), (V, mrk), (A, maE)],
                                  lambda t, kt=kt, qi_=qi_, bank=bank: t.matmul(
                                      out=pb[bank][:, 0:256],
                                      lhsT=khb[:, 128 * kt:128 * (kt + 1)],
                                      rhs=qhb[:, 256 * qi_:256 * (qi_ + 1)],
                                      start=True, stop=True))
                        maE = step("scalar", [(T, mt), (V, mrk)],
                                   lambda a, kt=kt, bank=bank: a.activation(
                                       out=exps[:, 256 * kt:256 * (kt + 1)],
                                       in_=pb[bank][:, 0:256], func=AF.Square,
                                       scale=rk[:, kt:kt + 1]))

                    def fmask(v, qi_=qi_):
                        for r in range(4):
                            kt = 4 * qi_ + r
                            v.tensor_tensor(out=exps[:, 256 * kt:256 * (kt + 1)],
                                            in0=exps[:, 256 * kt:256 * (kt + 1)],
                                            in1=masks[:, 256 * r:256 * (r + 1)],
                                            op=ALU.mult)
                            v.drain()
                    mv = step("vector", [(A, maE), (LI, m_ident)], fmask)

                    def fpv(t, ntr=ntr):
                        for kt in range(ntr):
                            t.matmul(out=pb[2][:, 0:256],
                                     lhsT=vaug[:, 128 * kt:128 * (kt + 1)],
                                     rhs=exps[:, 256 * kt:256 * (kt + 1)],
                                     start=(kt == 0), stop=(kt == ntr - 1))
                            t.matmul(out=pb[3][:, 0:256], lhsT=ones_bf[:],
                                     rhs=exps[:, 256 * kt:256 * (kt + 1)],
                                     start=(kt == 0), stop=(kt == ntr - 1))
                    mt = step("tensor", [(V, mv), (V, mvV)], fpv)

                    def fno(v, qi_=qi_):
                        v.scalar_tensor_tensor(out=muq[0:1, 0:256],
                                               in0=zrow[0:1, 256 * qi_:256 * (qi_ + 1)],
                                               scalar=EPS, in1=pb[3][0:1, 0:256],
                                               op0=ALU.mult, op1=ALU.add)
                        v.drain()
                        v.reciprocal(out=muq[0:1, 0:256], in_=muq[0:1, 0:256])
                        v.drain()
                        v.tensor_copy(out=st0[0:1, 0:256], in_=muq[0:1, 0:256])
                    mv2 = step("vector", [(T, mt)], fno)
                    mt2 = step("tensor", [(V, mv2)],
                               lambda t: t.matmul(out=pb[4][:, 0:256],
                                                  lhsT=ones_bf[:],
                                                  rhs=st0[:, 0:256],
                                                  start=True, stop=True))
                    mv3 = step("vector", [(T, mt2)],
                               lambda v: v.tensor_copy(out=rkst[:, 0:256],
                                                       in_=pb[4][:, 0:256]))

                    def fctx(v, h=h, qi_=qi_):
                        import os as _o
                        sl2 = ctxb[:, NT * h + 256 * qi_:NT * h + 256 * (qi_ + 1)]
                        m2 = _o.environ.get("KCTX", "mult")
                        if m2 == "copy":
                            v.tensor_copy(out=sl2, in_=pb[2][:, 0:256])
                        elif m2 == "div":
                            v.tensor_copy(out=sl2, in_=rkst[:, 0:256])
                        else:
                            v.tensor_tensor(out=sl2, in0=pb[2][:, 0:256],
                                            in1=rkst[:, 0:256], op=ALU.mult)
                    mctx = step("vector", [(V, mv3)], fctx)
            return mctx

        m_ctx1 = l1_mix()
        dump_sb("d_ctx1", ctxb, [(V, m_ctx1)])
        dump_sb("d_rk", rk[:], [(V, m_ctx1)])
        dump_sb("d_zr", zrow[0:1, :], [(V, m_ctx1)])
        dump_sb("d_att", exps[:, 0:4096], [(V, m_ctx1)])
        dump_sb("d_div", rkst[:, 0:256], [(V, m_ctx1)])
        dump_sb("d_vaug", vaug, [(V, m_ctx1)])
        dump_sb("d_khb", khb, [(V, m_ctx1)])

        def l1_out():
            mE = V.n
            for j in range(16):
                mw = dma1("gpsimd", [(T, T.n)], wgs[:, 0:2048], w1_in[3][j])

                def fmm(t, j=j):
                    for grp in range(2):
                        for k in range(16):
                            t.matmul(out=pb[grp][:],
                                     lhsT=wgs[:, 128 * k:128 * (k + 1)],
                                     rhs=ctxb[:, NT * k + 512 * grp:NT * k + 512 * (grp + 1)],
                                     start=(k == 0), stop=(k == 15))
                mt = step("tensor", [(LW, mw), (V, m_ctx1), (A, A.n), (V, V.n)], fmm)

                def fev(v, j=j):
                    for grp in range(2):
                        v.tensor_tensor(
                            out=dff[:, NT * j + 512 * grp:NT * j + 512 * (grp + 1)],
                            in0=dff[:, NT * j + 512 * grp:NT * j + 512 * (grp + 1)],
                            in1=pb[grp][:], op=ALU.add)
                mE = step("vector", [(T, mt), (V, m_dff1), (LI, LI.n)], fev)
            mo = 0
            for i in range(8):
                mlx = dma1("sync", [(V, V.n), (A, A.n), (LI, m_x1)], xt[:],
                           x1own[128 * i:128 * (i + 1), :])
                mE2 = V.n
                for b in range(2):
                    def ftr(t, i=i, b=b):
                        for j in range(8):
                            k = 8 * b + j
                            t.transpose(out=ptr[:, 128 * j:128 * (j + 1)],
                                        in_=dff[:, NT * k + 128 * i:NT * k + 128 * (i + 1)],
                                        identity=ident[:])
                    mt = step("tensor", [(V, mE), (V, V.n)], ftr)

                    def fevd(v, b=b):
                        for j in range(8):
                            k = 8 * b + j
                            v.tensor_copy(out=dtm[:, 128 * k:128 * (k + 1)],
                                          in_=ptr[:, 128 * j:128 * (j + 1)])
                    mE2 = step("vector", [(T, mt)], fevd)
                mv = step("vector", [(V, mE2), (LI, mlx)],
                          lambda v: v.tensor_tensor(out=xt[:], in0=xt[:],
                                                    in1=dtm, op=ALU.add))
                mo = dma1("sync", [(V, mv)], yo[128 * i:128 * (i + 1), :], xt[:])

            def ffin(s):
                s.wait_ge(LI.h, mo)
            blk.sync(ffin)
            return mo

        l1_out()

    return nc


def _tile_wT(w_t):
    out = np.empty((16, 128, 2048), np.float32)
    for j in range(16):
        for k in range(16):
            out[j][:, 128 * k:128 * (k + 1)] = w_t[128 * k:128 * (k + 1),
                                                   128 * j:128 * (j + 1)]
    return out


_NC_CACHE = {}


def kernel(**inputs):
    ii = {k: np.asarray(v) for k, v in inputs.items()}
    x = ii["L0_attn_hidden_states"].astype(np.float32)

    wA = [np.ascontiguousarray(ii[f"L0_attn_{n}_weight"].reshape(16, 128, D))
          .astype(np.float32) for n in ("q", "k", "v", "o")]
    wB = []
    for n in ("q", "k", "v", "o"):
        wt = ii[f"L0_attn_{n}_weight"].T
        t = np.empty((16, 4, 128, 512), np.float32)
        for k in range(16):
            for g in range(4):
                t[k, g] = wt[128 * k:128 * (k + 1), 512 * g:512 * (g + 1)]
        wB.append(t)
    l0b = [ii[f"L0_attn_{n}_bias"].astype(np.float32) for n in ("q", "k", "v", "o")]

    idx = np.empty(8192, np.int64)
    pos = 0
    for sl in range(16):
        for r in range(4):
            for i in range(128):
                idx[pos] = 512 * sl + 4 * i + r
                pos += 1
    wg = ii["L0_ffn_w_gate"][:, idx]
    wu = ii["L0_ffn_w_up"][:, idx]
    wd = ii["L0_ffn_w_down"][idx, :]

    wg_t = np.ascontiguousarray(
        wg.reshape(16, 128, 16, 512).transpose(2, 1, 0, 3).reshape(16, 128, 8192)
    ).astype(np.float32)
    wu_t = np.ascontiguousarray(
        wu.reshape(16, 128, 16, 512).transpose(2, 1, 0, 3).reshape(16, 128, 8192)
    ).astype(np.float32)
    wd_t = np.ascontiguousarray(
        wd.reshape(64, 128, 16, 128).transpose(2, 1, 0, 3).reshape(16, 128, 8192)
    ).astype(np.float32)

    w1 = [_tile_wT(ii[f"L1_attn_W{n}"].T) for n in ("q", "k", "v", "o")]
    wf1 = _tile_wT(ii["L1_ffn_W"].T)
    laT = ii["L1_ffn_lora_A"].reshape(128, D).T
    laT_t = np.ascontiguousarray(
        laT.reshape(16, 128, 128).transpose(1, 0, 2).reshape(128, 2048)
    ).astype(np.float32)
    lb = ii["L1_ffn_lora_B"]
    Bst = np.zeros((128, D), np.float32)
    for e in range(8):
        for r2 in range(16):
            Bst[16 * e + r2] = lb[e, :, r2]
    lbS_t = np.ascontiguousarray(Bst)
    gwT_full = np.zeros((D, 128), np.float32)
    gwT_full[:, 0:8] = ii["L1_ffn_gate_w"].T
    gwT_t = np.ascontiguousarray(
        gwT_full.reshape(16, 128, 128).transpose(1, 0, 2).reshape(128, 2048))

    ident = np.eye(128, dtype=ml_dtypes.bfloat16)
    sel8np = np.zeros((128, 128), np.float32)
    for e in range(8):
        sel8np[e, 16 * e:16 * (e + 1)] = 1.0
    sel8np = sel8np.astype(ml_dtypes.bfloat16)

    if "nc" not in _NC_CACHE:
        _NC_CACHE["nc"] = build_nc()
    nc = _NC_CACHE["nc"]

    in_maps = []
    for c in range(8):
        s, par = c // 2, c % 2
        gids = [2 * i + par for i in range(4)]
        xown = np.ascontiguousarray(
            np.concatenate([x[s, 256 * g:256 * (g + 1), :] for g in gids], axis=0))
        kk = np.arange(128)[:, None]
        qq = np.arange(256)[None, :]
        mask1 = np.stack([(128 * r + kk <= 256 * par + qq) for r in range(4)]) \
            .astype(ml_dtypes.bfloat16)
        im = {"xo": xown, "wg_t": wg_t, "wu_t": wu_t, "wd_t": wd_t,
              "gam": ii["L1_attn_gamma"].astype(np.float32),
              "bet": ii["L1_attn_beta"].astype(np.float32),
              "wf1": wf1, "b1": ii["L1_ffn_b"].astype(np.float32),
              "laT": laT_t, "lbS": lbS_t, "gwT": gwT_t,
              "gb": ii["L1_ffn_gate_b"].astype(np.float32),
              "ident": ident, "mask1": mask1, "sel8": sel8np}
        for i in range(4):
            im[f"l0wA{i}"] = wA[i]
            im[f"l0wB{i}"] = wB[i]
            im[f"l0b{i}"] = l0b[i]
            im[f"l1w{i}"] = w1[i]
        in_maps.append(im)

    r = run_bass_kernel_spmd(nc, in_maps, core_ids=list(range(8)))

    out = np.empty((4, 2048, D), np.float32)
    for c in range(8):
        s, par = c // 2, c % 2
        y = r.results[c]["yo"]
        for i, g in enumerate([2 * i + par for i in range(4)]):
            out[s, 256 * g:256 * (g + 1), :] = y[256 * i:256 * (i + 1), :]
    return out
